# version 16
# baseline (speedup 1.0000x reference)
import numpy as np
from contextlib import ExitStack

import ml_dtypes
import concourse.bass as bass
import concourse.mybir as mybir
from concourse.bass_utils import run_bass_kernel_spmd

F32 = mybir.dt.float32
BF16 = mybir.dt.bfloat16
FP8 = mybir.dt.float8e4
AF = mybir.ActivationFunctionType
ALU = mybir.AluOpType
AX = mybir.AxisListType.X

B, N, E, H, D, II = 4, 2048, 2048, 16, 128, 5632
TOK = 1024
NT = N // 128
OT = TOK // 128
NC = E // 128
NJ = II // 128
CEXP = 18.0
DEBUG = False
LIMIT = None
MARKS = {}

ENGINES = ("tensor", "vector", "scalar", "gpsimd", "sync")
NPOOL = {"sync": 10, "gpsimd": 4, "scalarq": 2}
SEMS = list(ENGINES) + [f"{q}{i}" for q, n in NPOOL.items() for i in range(n)]


class Plan:
    def __init__(self):
        self.items = {e: [] for e in ENGINES}
        self.cnt = {s: 0 for s in SEMS}
        self.rr = {q: 0 for q in NPOOL}
        self.seq = 0

    def mark(self, name):
        MARKS[name] = self.seq

    def emit(self, eng, fn, after=(), sem=None, amt=1):
        sem = sem or eng
        waits = {}
        for ev in after:
            if ev is None:
                continue
            s, vv = ev
            waits[s] = max(waits.get(s, 0), vv)
        self.cnt[sem] += amt
        self.seq += 1
        self.items[eng].append((self.seq, fn, sorted(waits.items()), sem, amt))
        return (sem, self.cnt[sem])

    def dma(self, q, fn, after=()):
        sem = f"{q}{self.rr[q]}"
        self.rr[q] = (self.rr[q] + 1) % NPOOL[q]
        prev = self.cnt[sem]
        extra = ((sem, prev),) if prev else ()
        eng = "scalar" if q == "scalarq" else q
        return self.emit(eng, fn, tuple(after) + extra, sem=sem, amt=16)

    def barrier(self, eng, after):
        waits = {}
        for ev in after:
            if ev is None:
                continue
            s, vv = ev
            waits[s] = max(waits.get(s, 0), vv)
        self.seq += 1
        self.items[eng].append((self.seq, None, sorted(waits.items()), None, 0))


def build_nc():
    nc = bass.Bass("TRN2", target_bir_lowering=False)
    p = Plan()

    xseq = nc.dram_tensor("xseq", [N, E], F32, kind="ExternalInput")
    xown = nc.dram_tensor("xown", [TOK, E], F32, kind="ExternalInput")
    selv_in = nc.dram_tensor("selvec", [128, 2], F32, kind="ExternalInput")
    wqT = nc.dram_tensor("wqT", [E, E], F32, kind="ExternalInput")
    wkvT = nc.dram_tensor("wkvT", [E, 256], F32, kind="ExternalInput")
    wkvhgT = nc.dram_tensor("wkvhgT", [E, 64], F32, kind="ExternalInput")
    woutT_in = nc.dram_tensor("woutT", [E, E], F32, kind="ExternalInput")
    wg_nat = nc.dram_tensor("wg_nat", [II, E], F32, kind="ExternalInput")
    wu_nat = nc.dram_tensor("wu_nat", [II, E], F32, kind="ExternalInput")
    wd_nat = nc.dram_tensor("wd_nat", [E, II], F32, kind="ExternalInput")
    t_qc = nc.dram_tensor("t_qc", [64, N], BF16, kind="ExternalInput")
    t_qs = nc.dram_tensor("t_qs", [64, N], BF16, kind="ExternalInput")
    t_kc = nc.dram_tensor("t_kc", [64, N], BF16, kind="ExternalInput")
    t_ks = nc.dram_tensor("t_ks", [64, N], BF16, kind="ExternalInput")
    identf_in = nc.dram_tensor("identf", [128, 128], F32, kind="ExternalInput")
    diag_in = nc.dram_tensor("diagmask", [128, 128], BF16, kind="ExternalInput")
    onescol_in = nc.dram_tensor("onescol", [128, 1], BF16, kind="ExternalInput")
    onesrow_in = nc.dram_tensor("onesrow", [1, 128], F32, kind="ExternalInput")

    qwd_scr = nc.dram_tensor("qwd_scr", [E, II], BF16, kind="Internal")
    qscr = nc.dram_tensor("qscr", [E, N], BF16, kind="Internal")
    lnscr = nc.dram_tensor("lnscr", [N, E], BF16, kind="Internal")
    x2scr = nc.dram_tensor("x2scr", [TOK, E], F32, kind="Internal")
    qx_nat = nc.dram_tensor("qx_nat", [TOK, E], BF16, kind="Internal")
    qt_nat = nc.dram_tensor("qt_nat", [TOK, II], BF16, kind="Internal")
    if DEBUG:
        dbg_den = nc.dram_tensor("dbg_den", [32, 1024], F32, kind="Internal")
        dbg_ost = nc.dram_tensor("dbg_ost", [4096, 1024], BF16, kind="Internal")
        dbg_p = nc.dram_tensor("dbg_p", [128, 2048], BF16, kind="Internal")
        dbg_oo = nc.dram_tensor("dbg_oo", [128, H * 1024], BF16, kind="Internal")
        dbg_wo = nc.dram_tensor("dbg_wo", [128, NC * E], BF16, kind="Internal")
        dbg_x2 = nc.dram_tensor("dbg_x2", [TOK, E], F32, kind="Internal")
        dbg_ps = nc.dram_tensor("dbg_ps", [128, E], F32, kind="Internal")
        dbg_xs = nc.dram_tensor("dbg_xs", [128, E], F32, kind="Internal")
    out = nc.dram_tensor("out", [TOK, E], F32, kind="ExternalOutput")

    es0 = ExitStack()
    es1 = ExitStack()
    es1b = ExitStack()
    es1c = ExitStack()
    es_o = ExitStack()
    es2 = ExitStack()
    es_qx = ExitStack()
    es3 = ExitStack()
    es4 = ExitStack()
    es5 = ExitStack()
    es_ps1 = ExitStack()
    es_ps2 = ExitStack()
    es_ps3 = ExitStack()
    es_ps4 = ExitStack()
    es_ps6 = ExitStack()
    es_psB = ExitStack()

    SCR = es0.enter_context(nc.sbuf_tensor("SCR", [128, 8192], F32))
    stats = es0.enter_context(nc.sbuf_tensor("stats", [128, 64], F32))
    cvec = es0.enter_context(nc.sbuf_tensor("cvec", [128, 8], F32))
    selvec = es0.enter_context(nc.sbuf_tensor("selvec_sb", [128, 2], F32))
    identb = es0.enter_context(nc.sbuf_tensor("identb", [128, 128], BF16))

    def R(i):
        return SCR[:, 2048 * i:2048 * (i + 1)]

    ln1fm = es1.enter_context(nc.sbuf_tensor("ln1fm", [128, NC * N], BF16))
    wqbf = es1.enter_context(nc.sbuf_tensor("wqbf", [128, NC * E], BF16))
    wkvhgbf = es1.enter_context(nc.sbuf_tensor("wkvhgbf", [128, NC * 64], BF16))
    lnq = es1.enter_context(nc.sbuf_tensor("lnq", [128, 4096], BF16))
    wkvbf = lnq
    rqc = es1.enter_context(nc.sbuf_tensor("rqc", [64, N], BF16))
    rqs = es1.enter_context(nc.sbuf_tensor("rqs", [64, N], BF16))
    rkc = es1.enter_context(nc.sbuf_tensor("rkc", [64, N], BF16))
    rks = es1.enter_context(nc.sbuf_tensor("rks", [64, N], BF16))
    identf = es1.enter_context(nc.sbuf_tensor("c_identf", [128, 128], F32))

    kpfm = es1b.enter_context(nc.sbuf_tensor("kpfm", [128, N], BF16, side="right"))
    vppm = es1b.enter_context(nc.sbuf_tensor("vppm", [128, N], BF16, side="right"))
    khvh = es1b.enter_context(nc.sbuf_tensor("khvh", [128, NT * 32], F32, side="right"))

    PBIG = es_psB.enter_context(nc.psum_tensor("PBIG", [128, 3072], F32))

    def dma(q, dst, src, after=()):
        return p.dma(q, lambda e, d=dst, s=src: e.dma_start(out=d, in_=s), after)

    def mm(out_ap, lhsT, rhs, start, stop, after=()):
        return p.emit("tensor",
                      lambda e, o=out_ap, l=lhsT, r=rhs, st=start, sp=stop:
                      e.matmul(o, l, r, start=st, stop=sp, skip_group_check=True),
                      after)

    def mmN(out_ap, lhsT, rhs, ncols, start, stop, after=(), nmax=512):
        base = out_ap.offset
        ev = None
        c0 = 0
        while c0 < ncols:
            w = min(nmax, ncols - c0, 512 - ((base + c0) % 512))
            ev = mm(out_ap[:, c0:c0 + w], lhsT, rhs[:, c0:c0 + w], start, stop,
                    after if c0 == 0 else ())
            c0 += w
        return ev

    def tr(out_ap, in_ap, ident, after=()):
        return p.emit("tensor", lambda e, o=out_ap, i=in_ap, d=ident: e.transpose(o, i, d), after)

    def v(fn, after=()):
        return p.emit("vector", fn, after)

    def a(fn, after=()):
        return p.emit("scalar", fn, after)

    ev_tabs = []
    for dst, srct in ((rqc, t_qc), (rqs, t_qs), (rkc, t_kc), (rks, t_ks),
                      (identf, identf_in), (selvec, selv_in)):
        ev_tabs.append(dma("sync", dst[:, :], srct[:, :]))
    ev_consts = ev_tabs
    ev = None
    for col, val in enumerate((-CEXP, 0.0, 1.0, 1e-5, 1.0 / E, -0.5, -1.0)):
        ev = v(lambda e, c=col, x=val: e.memset(cvec[:, c:c + 1], x), after=[ev])
    ev_cvec = ev
    ev_identb = v(lambda e: e.tensor_copy(identb[:, :], identf[:, :]),
                  after=[ev_tabs[4]])
    c_bias = cvec[:, 0:1]
    c_zero = cvec[:, 1:2]
    c_one = cvec[:, 2:3]
    c_eps = cvec[:, 3:4]
    c_invE = cvec[:, 4:5]
    c_mhalf = cvec[:, 5:6]
    c_mone = cvec[:, 6:7]

    ev_wq = []
    for ci in range(NC):
        ev_wq.append(dma("gpsimd", wqbf[:, 2048 * ci:2048 * (ci + 1)],
                         wqT[128 * ci:128 * (ci + 1), :]))
    ev_wkvhg = []
    for ci in range(NC):
        ev_wkvhg.append(dma("gpsimd", wkvhgbf[:, 64 * ci:64 * (ci + 1)],
                            wkvhgT[128 * ci:128 * (ci + 1), :]))

    xslot = [R(0), R(1)]
    lnslot = [lnq[:, 0:2048], lnq[:, 2048:4096]]
    ev_xfree = [None, None]
    ev_lnfree = [None, None]
    ev_lnst = [None] * NT
    ev_sq_prev = None
    ev_rs = None
    for t in range(NT):
        sl = t % 2
        ev_x = []
        for hh in range(2):
            ev_x.append(dma("sync", xslot[sl][:, 1024 * hh:1024 * (hh + 1)],
                            xseq[128 * t:128 * (t + 1), 1024 * hh:1024 * (hh + 1)],
                            after=[ev_xfree[sl]]))
        ev_red = v(lambda e, t=t, sl=sl: e.reduce_sum(
            stats[:, 32 + t:33 + t], xslot[sl], AX), after=ev_x)
        ev_nm = v(lambda e, t=t: e.tensor_scalar(
            stats[:, t:t + 1], stats[:, 32 + t:33 + t], -1.0 / E, None, ALU.mult),
            after=[ev_red])
        ev_sq_prev = a(lambda e, t=t, sl=sl: e.activation(
            R(2), xslot[sl], AF.Square, bias=stats[:, t:t + 1], scale=c_one,
            accum_out=stats[:, 32 + t:33 + t]),
            after=[ev_nm, ev_cvec, ev_sq_prev])
        ev_ln = a(lambda e, t=t: e.activation(stats[:, 48 + t:49 + t],
                                              stats[:, 32 + t:33 + t], AF.Ln,
                                              bias=c_eps, scale=c_invE),
                  after=[ev_sq_prev])
        ev_rs = a(lambda e, t=t: e.activation(stats[:, 16 + t:17 + t],
                                              stats[:, 48 + t:49 + t], AF.Exp,
                                              bias=c_zero, scale=c_mhalf),
                  after=[ev_ln])
        ev_app = v(lambda e, t=t, sl=sl: e.tensor_scalar(
            lnslot[sl], xslot[sl], stats[:, t:t + 1], stats[:, 16 + t:17 + t],
            ALU.add, ALU.mult),
            after=[ev_rs, ev_lnfree[sl]])
        ev_xfree[sl] = ev_app
        ev_lnst[t] = dma("sync", lnscr[128 * t:128 * (t + 1), :], lnslot[sl],
                         after=[ev_app])
        ev_lnfree[sl] = ev_lnst[t]
    ev_sq_last = ev_sq_prev
    ev_rstd = ev_rs
    ev_ln1T = []
    for ci in range(NC):
        ev_ln1T.append(p.dma("sync",
                             lambda e, ci=ci: e.dma_start(
                                 out=ln1fm[:, N * ci:N * (ci + 1)],
                                 in_=lnscr[:, 128 * ci:128 * (ci + 1)], transpose=True),
                             after=ev_lnst if ci == 0 else ()))
    ev_ln1_done = ev_ln1T[-1]

    p.mark("PH1")
    ev_wkv = []
    for ci in range(NC):
        ev_wkv.append(dma("gpsimd", wkvbf[:, 256 * ci:256 * (ci + 1)],
                          wkvT[128 * ci:128 * (ci + 1), :],
                          after=ev_lnst if ci == 0 else ()))
    ev_kp = None
    for ci in range(NC):
        ev_kp = mmN(PBIG[:, 0:2048], wkvbf[:, 256 * ci:256 * ci + 128],
                    ln1fm[:, N * ci:N * (ci + 1)], N, start=(ci == 0), stop=(ci == NC - 1),
                    after=ev_ln1T + ev_wkv + ev_wkvhg if ci == 0 else ())
    q1, q2 = PBIG[0:64, 0:2048], PBIG[64:128, 0:2048]
    s1, s2 = R(3)[0:64, :], R(2)[0:64, :]
    s3, s4 = R(3)[64:128, :], R(2)[64:128, :]

    def rope_into(dst_sb, ctab, stab, after0):
        e1 = v(lambda e: e.tensor_tensor(s1, q1, ctab[:, :], ALU.mult), after=after0)
        e2 = v(lambda e: e.tensor_tensor(s2, q2, stab[:, :], ALU.mult), after=[e1])
        e4 = v(lambda e: e.tensor_tensor(s3, q2, ctab[:, :], ALU.mult), after=[e2])
        e5 = v(lambda e: e.tensor_tensor(s4, q1, stab[:, :], ALU.mult), after=[e4])
        e3 = v(lambda e: e.tensor_tensor(q1, s1, s2, ALU.subtract), after=[e5])
        e6 = v(lambda e: e.tensor_tensor(q2, s3, s4, ALU.add), after=[e3])
        e7 = v(lambda e, d=dst_sb: e.tensor_copy(d, PBIG[:, 0:2048]), after=[e6])
        return e7

    ev_kpfm = rope_into(kpfm[:, :], rkc, rks, [ev_kp] + ev_consts)

    ev_vp = None
    for grp in range(4):
        evl = None
        for q in range(4):
            t = 4 * grp + q
            for ci in range(NC):
                evl = mm(PBIG[:, 2048 + 128 * q:2048 + 128 * (q + 1)],
                         ln1fm[:, N * ci + 128 * t:N * ci + 128 * (t + 1)],
                         wkvbf[:, 256 * ci + 128:256 * (ci + 1)],
                         start=(ci == 0), stop=(ci == NC - 1),
                         after=ev_ln1T + [ev_vp] if (ci == 0 and q == 0) else ())
        ev_vp = v(lambda e, g=grp: e.tensor_copy(vppm[:, 512 * g:512 * (g + 1)],
                                                 PBIG[:, 2048:2560]), after=[evl])
    ev_vp_done = ev_vp

    ev_g = None
    for ci in range(NC):
        ev_g = mmN(PBIG[0:32, 0:2048], wkvhgbf[:, 64 * ci + 32:64 * (ci + 1)],
                   ln1fm[:, N * ci:N * (ci + 1)], N, start=(ci == 0), stop=(ci == NC - 1),
                   after=[ev_kpfm] if ci == 0 else ())
    fsig, zer, prod, kvh_sb = R(0)[0:32, :], R(1)[0:32, :], R(2)[0:32, :], R(3)[0:32, :]
    one_, kvhw, kvhc = R(0)[0:32, :], R(1)[0:32, :], R(3)[0:32, :]
    ev_sig = a(lambda e: e.activation(fsig, PBIG[0:32, 0:2048], AF.Sigmoid),
               after=[ev_g, ev_xfree[0], ev_xfree[1], ev_sq_last, ev_kpfm])
    ev_z = v(lambda e: e.memset(zer, 0.0), after=[ev_xfree[1], ev_kpfm])
    ev_kvhmm = None
    for ci in range(NC):
        ev_kvhmm = mmN(PBIG[0:32, 0:2048], wkvhgbf[:, 64 * ci:64 * ci + 32],
                       ln1fm[:, N * ci:N * (ci + 1)], N, start=(ci == 0), stop=(ci == NC - 1),
                       after=[ev_sig] if ci == 0 else ())
    ev_kvh = v(lambda e: e.tensor_copy(kvh_sb, PBIG[0:32, 0:2048]),
               after=[ev_kvhmm, ev_kpfm])
    ev_prod = v(lambda e: e.tensor_tensor_scan(prod, fsig, zer, 1.0, ALU.mult, ALU.add),
                after=[ev_sig, ev_z, ev_kvh, ev_sq_last])
    ev_w0 = v(lambda e: e.tensor_copy(kvhw[:, 0:1], kvh_sb[:, 0:1]), after=[ev_prod])
    ev_w1 = v(lambda e: e.tensor_tensor(kvhw[:, 1:N], kvh_sb[:, 1:N], prod[:, 0:N - 1],
                                        ALU.mult), after=[ev_w0])
    ev_o = v(lambda e: e.memset(one_, 1.0), after=[ev_prod])
    ev_kvhc = v(lambda e: e.tensor_tensor_scan(kvhc, one_, kvhw, 0.0, ALU.mult, ALU.add),
                after=[ev_w1, ev_o])
    ev_ts = [None, None]
    for t in range(NT):
        sl = t % 2
        ps = PBIG[:, 2048 + 512 * sl:2080 + 512 * sl]
        evt = tr(ps, kvhc[:, 128 * t:128 * (t + 1)], identf[0:32, 0:32],
                 after=[ev_kvhc, ev_ts[sl], ev_vp_done])
        ev_ts[sl] = v(lambda e, t=t, ps=ps: e.tensor_copy(khvh[:, 32 * t:32 * (t + 1)], ps),
                      after=[evt])
    ev_khvh = ev_ts[(NT - 1) % 2]
    ev_khvh2 = ev_ts[NT % 2]

    qstage = [lnq[:, 0:2048], lnq[:, 2048:4096]]
    ev_qst = [ev_vp_done, ev_vp_done]
    ev_qrope_scr = ev_kpfm
    ev_qmm_prev = [ev_kvh, ev_khvh, ev_khvh2]
    ev_qscr_wr = [None] * H
    for h in range(H):
        hs = h % 2
        ev_qmm = None
        for ci in range(NC):
            ev_qmm = mmN(PBIG[:, 0:2048], wqbf[:, 2048 * ci + 128 * h:2048 * ci + 128 * (h + 1)],
                         ln1fm[:, N * ci:N * (ci + 1)], N, start=(ci == 0), stop=(ci == NC - 1),
                         after=(ev_qmm_prev + ev_wq) if ci == 0 else ())
        e7 = rope_into(qstage[hs], rqc, rqs, [ev_qmm, ev_qrope_scr, ev_qst[hs]])
        ev_qrope_scr = e7
        ev_qmm_prev = [e7]
        ev_qscr_wr[h] = dma("sync", qscr[128 * h:128 * (h + 1), :], qstage[hs], after=[e7])
        ev_qst[hs] = ev_qscr_wr[h]

    p.mark("PH2")
    es1.close()
    es_ps1.close()
    es_ps2_ = es_ps2
    PDEN = es_ps2_.enter_context(nc.psum_tensor("PDEN", [1, 1024], F32))
    o_own = es_o.enter_context(nc.sbuf_tensor("o_own", [128, H * 1024], BF16))
    qfm2 = es1c.enter_context(nc.sbuf_tensor("qfm2", [128, 2 * N], BF16))
    ptile = es1c.enter_context(nc.sbuf_tensor("ptile", [128, 2048], BF16))
    vvh = es1c.enter_context(nc.sbuf_tensor("vvh", [128, 256], BF16))
    diag = es1c.enter_context(nc.sbuf_tensor("c_diag", [128, 128], BF16))
    onescol = es1c.enter_context(nc.sbuf_tensor("c_onescol", [128, 1], BF16))
    onesrow = es1c.enter_context(nc.sbuf_tensor("c_onesrow", [1, 128], F32))
    muxstg = es1c.enter_context(nc.sbuf_tensor("muxstg", [128, 1024], BF16))
    ostg = es1c.enter_context(nc.sbuf_tensor("ostg", [128, 2048], BF16))
    ev_lc = []
    for dst, srct in ((diag, diag_in), (onescol, onescol_in), (onesrow, onesrow_in)):
        ev_lc.append(dma("sync", dst[:, :], srct[:, :],
                         after=[ev_qscr_wr[H - 1]] if not ev_lc else ()))
    ev_lateconsts = ev_lc

    wnatD = es3.enter_context(nc.sbuf_tensor("wnatD", [128, II], F32))
    q8wD = es3.enter_context(nc.sbuf_tensor("q8wD", [128, II], FP8))
    qbwD = es3.enter_context(nc.sbuf_tensor("qbwD", [128, II], BF16))

    dstore = [None] * NC

    def make_ph4(jobs, wn, q8w, qbw, nslot, first):
        st = {"i": 0, "wn_free": [first[1]] * nslot, "q8w_free": first[0],
              "qbw_free": [first[1]] * nslot, "first": list(first)}

        def emit_unit():
            i = st["i"]
            if i >= len(jobs):
                return
            st["i"] = i + 1
            srcd, dstd, arr, ch, width = jobs[i]
            csl = i % nslot
            sc = 44 if csl == 0 else 56
            ev_ld = dma("sync", wn[csl][:, 0:width], srcd[128 * ch:128 * (ch + 1), :],
                        after=[st["wn_free"][csl]] + st["first"])
            st["first"] = []
            ev_rm = v(lambda e, w=width, c=csl, sc=sc: e.reduce_max(
                stats[:, sc:sc + 1], wn[c][:, 0:w], AX, apply_absolute_value=True),
                after=[ev_ld, ev_rstd])
            ev_rc = v(lambda e, sc=sc: e.reciprocal(stats[:, sc + 1:sc + 2],
                                                    stats[:, sc:sc + 1]), after=[ev_rm])
            ev_rc2 = v(lambda e, sc=sc: e.tensor_scalar(stats[:, sc + 2:sc + 3],
                                                        stats[:, sc + 1:sc + 2], 224.0, None,
                                                        ALU.mult), after=[ev_rc])
            ev_dq = v(lambda e, sc=sc: e.tensor_scalar(stats[:, sc + 3:sc + 4],
                                                       stats[:, sc:sc + 1], 1.0 / 224.0,
                                                       None, ALU.mult), after=[ev_rm])
            ev_q8 = a(lambda e, w=width, c=csl, sc=sc: e.activation(
                q8w[:, 0:w], wn[c][:, 0:w], AF.Identity, bias=c_zero,
                scale=stats[:, sc + 2:sc + 3]),
                after=[ev_rc2, st["q8w_free"], ev_cvec])
            ev_db = v(lambda e, w=width, c=csl, sc=sc: e.tensor_scalar(
                qbw[c][:, 0:w], q8w[:, 0:w], stats[:, sc + 3:sc + 4], None,
                ALU.mult), after=[ev_q8, ev_dq, st["qbw_free"][csl]])
            ev_st = dma("sync", dstd[128 * ch:128 * (ch + 1), :], qbw[csl][:, 0:width],
                        after=[ev_db])
            arr[ch] = ev_st
            st["wn_free"][csl] = ev_db
            st["q8w_free"] = ev_db
            st["qbw_free"][csl] = ev_st
        return st, emit_unit

    wd_jobs = [(wd_nat, qwd_scr, dstore, ch, II) for ch in range(NC)]
    ph4d_state, emit_wd_unit = make_ph4(
        wd_jobs, [wnatD[:, :]], q8wD, [qbwD[:, :]], 1,
        (ev_rstd, ev_qscr_wr[H - 1]))

    stA, stB = PBIG[:, 0:1024], PBIG[:, 1024:2048]
    Oreg = PBIG[:, 2048:3072]

    ev_dbg = [None, None]
    ev_dbg_ost = [None, None]
    ev_qld = [None, None]
    ev_qfree = [None, None]
    ev_st_free = [None, None]
    ev_pt_free = [None, None]
    ev_vvh_free = [None, None]
    ev_o_evac = None
    ev_den_free = None
    ev_rb_done = ev_qrope_scr
    ev_ostg_free = [None, None]
    ev_mux = [None]

    for h in range(min(2, H)):
        ev_qld[h % 2] = dma("sync", qfm2[:, N * (h % 2):N * (h % 2 + 1)],
                            qscr[128 * h:128 * (h + 1), :], after=[ev_qscr_wr[h]])

    for h in range(H):
        hs = h % 2
        qslot = qfm2[:, N * hs:N * (hs + 1)]
        ev_q_ready = ev_qld[hs]
        ev_on_pass = [None, None]
        for pas in range(2):
            cbase = 1024 * pas
            kts = [kt for kt in range(NT) if 128 * kt < cbase + 1024]
            ev_pv_last = None
            ev_den_last = None
            pend = [None]
            nkt = len(kts)

            def flush_pend():
                nonlocal ev_den_last, ev_pv_last
                if pend[0] is None:
                    return
                idx, ks, c0l, nc_, ev_msk, ev_vvhe = pend[0]
                pend[0] = None
                ev_den_last = mmN(PDEN[0:1, c0l:c0l + nc_], onescol[:, :],
                                  ptile[:, 1024 * ks:1024 * ks + nc_], nc_,
                                  start=(idx == 0), stop=(idx == nkt - 1),
                                  after=[ev_msk, ev_den_free])
                ev_pv_last = mmN(Oreg[:, c0l:c0l + nc_], vvh[:, 128 * ks:128 * (ks + 1)],
                                 ptile[:, 1024 * ks:1024 * ks + nc_], nc_,
                                 start=(idx == 0), stop=(idx == nkt - 1),
                                 after=[ev_msk, ev_vvhe, ev_o_evac])
                ev_pt_free[ks] = ev_pv_last
                ev_vvh_free[ks] = ev_pv_last

            for idx, kt in enumerate(kts):
                ks = kt % 2
                streg = stA if ks == 0 else stB
                c0g = max(cbase, 128 * kt)
                nc_ = cbase + 1024 - c0g
                c0l = c0g - cbase
                ev_st = mmN(streg[:, c0l:c0l + nc_], kpfm[:, 128 * kt:128 * (kt + 1)],
                            qslot[:, c0g:c0g + nc_], nc_, start=True, stop=True,
                            after=[ev_q_ready, ev_kpfm, ev_st_free[ks], ev_vp_done,
                                   ev_khvh, ev_khvh2] + ev_lateconsts)
                ev_exp = a(lambda e, ks=ks, nc_=nc_, sr=streg[:, c0l:c0l + nc_],
                           sc=khvh[:, 32 * kt + h:32 * kt + h + 1]:
                           e.activation(ptile[:, 1024 * ks:1024 * ks + nc_], sr, AF.Exp,
                                        bias=c_bias, scale=sc),
                           after=[ev_st, ev_pt_free[ks], ev_cvec])
                ev_st_free[ks] = ev_exp
                ev_msk = ev_exp
                if 128 * kt >= cbase:
                    ev_msk = v(lambda e, pt=ptile[:, 1024 * ks:1024 * ks + 128]:
                               e.tensor_tensor(pt, pt, diag[:, :], ALU.mult), after=[ev_exp])
                ev_vvhe = v(lambda e, ks=ks, kt=kt,
                            sc=khvh[:, 32 * kt + 16 + h:32 * kt + 17 + h]:
                            e.tensor_scalar(vvh[:, 128 * ks:128 * (ks + 1)],
                                            vppm[:, 128 * kt:128 * (kt + 1)], sc, None,
                                            ALU.mult),
                            after=[ev_vp_done, ev_khvh, ev_khvh2, ev_vvh_free[ks]]
                                   + ev_lateconsts)
                flush_pend()
                pend[0] = (idx, ks, c0l, nc_, ev_msk, ev_vvhe)
                if DEBUG and h == 0 and pas == 0 and kt == 0:
                    ev_pt_free[ks] = dma("sync", dbg_p[:, 0:1024],
                                         ptile[:, 0:1024], after=[ev_msk])
            flush_pend()
            if h + 2 < H and pas == 1:
                ev_qld[hs] = dma("sync", qslot, qscr[128 * (h + 2):128 * (h + 3), :],
                                 after=[ev_qscr_wr[h + 2], ev_pv_last])
                ev_qfree[hs] = ev_qld[hs]
            lnden = R(2)[0:1, 0:1024]
            rbrow = R(2)[0:1, 1024:2048]
            ev_lnd = a(lambda e, o=lnden: e.activation(o, PDEN[0:1, :], AF.Ln),
                       after=[ev_den_last, ev_sq_last, ev_kvhc])
            ev_rbr = a(lambda e, o=rbrow, i=lnden: e.activation(o, i, AF.Exp,
                                                                bias=cvec[0:1, 1:2],
                                                                scale=cvec[0:1, 6:7]),
                       after=[ev_lnd, ev_dbg[0]])
            ev_den_free = ev_lnd
            ev_rb1 = mm(PBIG[:, 0:512], onesrow[:, :], rbrow[0:1, 0:512], True, True,
                        after=[ev_rbr, ev_st_free[0]])
            ev_rb2 = mm(PBIG[:, 512:1024], onesrow[:, :], rbrow[0:1, 512:1024], True, True,
                        after=[ev_rbr])
            rbsb = R(3)[:, 0:1024]
            ev_rbe = v(lambda e: e.tensor_copy(rbsb, PBIG[:, 0:1024]),
                       after=[ev_rb1, ev_rb2, ev_rb_done])
            ev_on = v(lambda e, o=ostg[:, 1024 * pas:1024 * (pas + 1)]:
                      e.tensor_tensor(o, Oreg, rbsb, ALU.mult),
                      after=[ev_rbe, ev_pv_last, ev_mux[0], ev_ostg_free[pas],
                             ev_dbg_ost[pas]])
            ev_o_evac = ev_on
            ev_rb_done = ev_on
            ev_st_free[0] = ev_rbe
            ev_on_pass[pas] = ev_on
            if DEBUG:
                ev_dbg[0] = dma("sync", dbg_den[2 * h + pas:2 * h + pas + 1, :], rbrow,
                                after=[ev_rbr, ev_rbe])
                ev_dbg_ost[pas] = dma("sync",
                                      dbg_ost[128 * (2 * h + pas):128 * (2 * h + pas + 1), :],
                                      ostg[:, 1024 * pas:1024 * (pas + 1)], after=[ev_on])
            ev_ostg_free[pas] = ev_on
        ev_m1 = v(lambda e: e.tensor_scalar(muxstg[:, :], ostg[:, 0:1024],
                                            selvec[:, 0:1], None, ALU.mult),
                  after=[ev_on_pass[0]] + ev_consts)
        ev_m2 = v(lambda e, o=o_own[:, 1024 * h:1024 * (h + 1)]:
                  e.scalar_tensor_tensor(o, ostg[:, 1024:2048], selvec[:, 1:2], muxstg[:, :],
                                         ALU.mult, ALU.add),
                  after=[ev_m1, ev_on_pass[1]])
        ev_mux[0] = ev_m2
        ev_ostg_free[0] = ev_m2
        ev_ostg_free[1] = ev_m2
        if h + 2 >= H:
            ev_qfree[hs] = ev_pv_last
        emit_wd_unit()
    while ph4d_state["i"] < len(wd_jobs):
        emit_wd_unit()
    ev_ph4_all = ([ph4d_state["q8w_free"]] + ph4d_state["qbw_free"]
                  + ph4d_state["wn_free"] + [dstore[-1]])
    ev_attn_done = ev_mux[0]

    p.mark("ATTN")
    es3.close()
    es1c.close()
    es1b.close()
    es_ps2.close()
    woutbf = es2.enter_context(nc.sbuf_tensor("woutbf", [128, NC * E], BF16))
    qxfm = es_qx.enter_context(nc.sbuf_tensor("qxfm", [128, NC * TOK], BF16, side="right"))
    q8x = es2.enter_context(nc.sbuf_tensor("q8x", [128, 2048], FP8))
    lnstg2 = es2.enter_context(nc.sbuf_tensor("lnstg2", [128, 4096], BF16))
    lnstg = [lnstg2[:, 0:2048], lnstg2[:, 2048:4096]]

    if DEBUG:
        dma("sync", dbg_oo[:, :], o_own[:, :], after=[ev_attn_done])
    ev_wo = []
    for ci in range(NC):
        ev_wo.append(dma("gpsimd", woutbf[:, 2048 * ci:2048 * (ci + 1)],
                         woutT_in[128 * ci:128 * (ci + 1), :],
                         after=([ev_attn_done] + ev_ph4_all + ev_dbg_ost + ev_dbg)
                         if ci == 0 else ()))

    if DEBUG:
        dma("sync", dbg_wo[:, :], woutbf[:, :], after=ev_wo)
    ev_x2wr = [None] * OT
    ev_sqw = ev_rstd
    ev_xsl = [ev_attn_done, ev_attn_done]
    ev_wout_prev = [ev_attn_done]
    for T in range(OT):
        sl = T % 2
        ev_wmm = None
        for h in range(H):
            ev_wmm = mmN(PBIG[:, 0:2048], o_own[:, 1024 * h + 128 * T:1024 * h + 128 * (T + 1)],
                         woutbf[:, 2048 * h:2048 * (h + 1)], E,
                         start=(h == 0), stop=(h == H - 1),
                         after=(ev_wout_prev + [ev_wo[h], ev_attn_done]) if h == 0
                         else ([ev_wo[h]] if T == 0 else ()))
        ev_xl = dma("sync", xslot[sl], xown[128 * T:128 * (T + 1), :], after=[ev_xsl[sl]])
        if DEBUG and T == 0:
            evps = v(lambda e: e.tensor_copy(R(3), PBIG[:, 0:2048]), after=[ev_wmm, ev_rstd])
            evpsd = dma("sync", dbg_ps[:, :], R(3), after=[evps])
            evxsd = dma("sync", dbg_xs[:, :], xslot[sl], after=[ev_xl])
            ev_sqw = evpsd
            ev_xsl[sl] = evxsd
        ev_x2 = v(lambda e, xs_=xslot[sl]: e.tensor_tensor(R(2), PBIG[:, 0:2048], xs_, ALU.add),
                  after=[ev_wmm, ev_xl, ev_sqw, ev_xsl[sl]])
        ev_wout_prev = [ev_x2]
        ev_xsl[sl] = ev_x2
        ev_x2wr[T] = dma("sync", x2scr[128 * T:128 * (T + 1), :], R(2), after=[ev_x2])
        if DEBUG:
            ev_x2wr[T] = dma("sync", dbg_x2[128 * T:128 * (T + 1), :], R(2),
                             after=[ev_x2, ev_x2wr[T]])
        ev_red = v(lambda e, T=T: e.reduce_sum(stats[:, 32 + T:33 + T], R(2), AX),
                   after=[ev_x2])
        ev_nm = v(lambda e, T=T: e.tensor_scalar(stats[:, T:T + 1], stats[:, 32 + T:33 + T],
                                                 -1.0 / E, None, ALU.mult), after=[ev_red])
        ev_sqw = a(lambda e, T=T: e.activation(R(3), R(2), AF.Square, bias=stats[:, T:T + 1],
                                               scale=c_one, accum_out=stats[:, 32 + T:33 + T]),
                   after=[ev_nm, ev_sqw, ev_x2wr[T]])
    ev_lnv2 = a(lambda e: e.activation(stats[:, 48:56], stats[:, 32:40], AF.Ln,
                                       bias=c_eps, scale=c_invE), after=[ev_sqw])
    ev_rstd2 = a(lambda e: e.activation(stats[:, 16:24], stats[:, 48:56], AF.Exp,
                                        bias=c_zero, scale=c_mhalf), after=[ev_lnv2])

    scr_bf = SCR.bitcast(BF16)
    wnatG = [R(3), R(0)]
    qbwG = [scr_bf[:, 4096:6144], scr_bf[:, 6144:8192],
            scr_bf[:, 8192:10240], scr_bf[:, 10240:12288]]
    q8w0 = es2.enter_context(nc.sbuf_tensor("q8w0", [128, 2 * E], FP8))
    x2slot2 = es2.enter_context(nc.sbuf_tensor("x2slot", [128, 4096], F32))
    x2slot = [x2slot2[:, 0:2048], x2slot2[:, 2048:4096]]
    NJQ = II // 256
    NGU = 4 * NJQ
    ev_wlast = ev_wout_prev[0]
    gu = {"i": 0, "wn_free": [ev_rstd2, ev_wlast], "q8_free": [ev_wlast, ev_wlast],
          "dqg": [ev_rstd2, ev_rstd2],
          "qbw_free": [ev_wlast, ev_wlast, None, None, None, None],
          "db": [None] * NGU, "q8buf": q8w0}

    def emit_gu_unit():
        i = gu["i"]
        if i >= NGU:
            return
        gu["i"] = i + 1
        srcd = wg_nat if i % 2 == 0 else wu_nat
        ch = i // 2
        wsl = i % 2
        qsl = i % 6
        sc = 44 if wsl == 0 else 56
        q8b = gu["q8buf"][:, E * wsl:E * (wsl + 1)]
        ev_l = []
        for hh in range(2):
            ev_l.append(dma("sync", wnatG[wsl][:, 1024 * hh:1024 * (hh + 1)],
                            srcd[128 * ch:128 * (ch + 1), 1024 * hh:1024 * (hh + 1)],
                            after=[gu["wn_free"][wsl]]))
        ev_rm = v(lambda e, w=wsl, sc=sc: e.reduce_max(
            stats[:, sc:sc + 1], wnatG[w], AX, apply_absolute_value=True),
            after=ev_l + [ev_rstd])
        ev_rc = v(lambda e, sc=sc: e.reciprocal(stats[:, sc + 1:sc + 2],
                                                stats[:, sc:sc + 1]), after=[ev_rm])
        ev_rc2 = v(lambda e, sc=sc: e.tensor_scalar(stats[:, sc + 2:sc + 3],
                                                    stats[:, sc + 1:sc + 2], 224.0, None,
                                                    ALU.mult), after=[ev_rc])
        ev_dq = v(lambda e, sc=sc: e.tensor_scalar(stats[:, sc + 3:sc + 4],
                                                   stats[:, sc:sc + 1], 1.0 / 224.0,
                                                   None, ALU.mult),
                  after=[ev_rm, gu["dqg"][wsl]])
        ev_q8 = a(lambda e, w=wsl, sc=sc, qb=q8b: e.activation(
            qb, wnatG[w], AF.Identity, bias=c_zero,
            scale=stats[:, sc + 2:sc + 3]),
            after=[ev_rc2, gu["q8_free"][wsl], ev_cvec])
        ev_db = a(lambda e, qv=qbwG[qsl], sc=sc, qb=q8b: e.activation(
            qv, qb, AF.Identity, bias=c_zero,
            scale=stats[:, sc + 3:sc + 4]),
            after=[ev_q8, ev_dq, gu["qbw_free"][qsl]])
        gu["db"][i] = ev_db
        gu["wn_free"][wsl] = ev_q8
        gu["q8_free"][wsl] = ev_db
        gu["dqg"][wsl] = ev_db

    ev_q8_free = None
    ev_lnstg_free = [None, None]
    ev_x2sl = [None, None]
    ev_qxst = [None] * OT
    for T in range(OT):
        sl = T % 2
        ev_xl = dma("sync", x2slot[sl], x2scr[128 * T:128 * (T + 1), :],
                    after=([ev_x2wr[T], ev_x2sl[sl]]
                           + (([ev_attn_done] + ev_ph4_all + ev_dbg_ost + ev_dbg)
                              if T < 2 else [])))
        ev_ln2 = v(lambda e, T=T, sl=sl: e.tensor_scalar(R(2), x2slot[sl], stats[:, T:T + 1],
                                                         stats[:, 16 + T:17 + T],
                                                         ALU.add, ALU.mult),
                   after=[ev_xl, ev_rstd2, ev_q8_free])
        ev_x2sl[sl] = ev_ln2
        if T in (1, 3):
            emit_gu_unit()
        ev_rm = v(lambda e: e.reduce_max(stats[:, 40:41], R(2), AX,
                                         apply_absolute_value=True), after=[ev_ln2])
        ev_rc = v(lambda e: e.reciprocal(stats[:, 41:42], stats[:, 40:41]), after=[ev_rm])
        ev_rc2 = v(lambda e: e.tensor_scalar(stats[:, 42:43], stats[:, 41:42], 224.0, None,
                                             ALU.mult), after=[ev_rc])
        ev_dq = v(lambda e: e.tensor_scalar(stats[:, 43:44], stats[:, 40:41], 1.0 / 224.0,
                                            None, ALU.mult), after=[ev_rm])
        ev_q8 = a(lambda e: e.activation(q8x[:, :], R(2), AF.Identity, bias=c_zero,
                                         scale=stats[:, 42:43]),
                  after=[ev_rc2, ev_q8_free])
        ev_dqx = a(lambda e, sl=sl: e.activation(lnstg[sl], q8x[:, :], AF.Identity,
                                                 bias=c_zero, scale=stats[:, 43:44]),
                   after=[ev_q8, ev_dq, ev_lnstg_free[sl]])
        ev_q8_free = ev_dqx
        ev_qxst[T] = dma("sync", qx_nat[128 * T:128 * (T + 1), :], lnstg[sl],
                         after=[ev_dqx])
        ev_lnstg_free[sl] = ev_qxst[T]
    gu["qbw_free"][2] = ev_q8_free
    gu["qbw_free"][3] = ev_q8_free
    for _ in range(2):
        emit_gu_unit()
    ev_qxT = []
    for ci in range(NC):
        ev_qxT.append(p.dma("sync",
                            lambda e, ci=ci: e.dma_start(
                                out=qxfm[:, TOK * ci:TOK * (ci + 1)],
                                in_=qx_nat[:, 128 * ci:128 * (ci + 1)], transpose=True),
                            after=ev_qxst if ci == 0 else ()))
    ev_qx_done = ev_qxT[-1]

    p.mark("WOUT")
    ev_wout_last = ev_wout_prev[0]
    es2.close()
    es_o.close()

    p.mark("PH4")
    tstore = es4.enter_context(nc.sbuf_tensor("tstore", [128, OT * II], BF16))
    jbuf = es4.enter_context(nc.sbuf_tensor("jbuf", [128, 4 * 4096], BF16))
    jg = [jbuf[:, 0:4096], jbuf[:, 4096:8192]]
    ju = [jbuf[:, 8192:12288], jbuf[:, 12288:16384]]
    silstg = es4.enter_context(nc.sbuf_tensor("silstg", [128, 512], BF16))
    q8t = es4.enter_context(nc.sbuf_tensor("q8t", [128, II], FP8))
    q8wS = es4.enter_context(nc.sbuf_tensor("q8wS", [128, 2 * E], FP8))
    qbwE = es4.enter_context(nc.sbuf_tensor("qbwE", [128, 2 * E], BF16))
    ptsg = es_ps4.enter_context(nc.psum_tensor("ptsg", [128, 2048], BF16))
    stg = [ptsg[:, 0:1024], ptsg[:, 1024:2048]]

    qbwG.append(qbwE[:, 0:E])
    qbwG.append(qbwE[:, E:2 * E])
    gu["qbw_free"][4] = ev_qxT[0]
    gu["qbw_free"][5] = ev_qxT[0]
    gu["q8buf"] = q8wS
    gu["q8_free"] = [ev_qxT[0], ev_qxT[0]]

    ev_qx_all = ev_qxT
    ev_jg_free = [None, None]
    ev_ju_free = [None, None]
    ev_stg_free = [None, None]
    evac_ev = {}
    kb = [0]

    def emit_T_batch(Jt, w, k):
        ia = 4 * Jt + (0 if w == 0 else 1)
        ib = ia + 2
        qa, qb = qbwG[ia % 6], qbwG[ib % 6]
        psl = kb[0] % 2
        kb[0] += 1
        evt = None
        for m in range(4):
            ci = 4 * k + m
            aft = [gu["db"][ia], gu["db"][ib], ev_stg_free[psl], ev_identb] \
                if m == 0 else ()
            evt = tr(stg[psl][:, 256 * m:256 * m + 128],
                     qa[:, 128 * ci:128 * (ci + 1)], identb[:, :], after=aft)
            evt = tr(stg[psl][:, 256 * m + 128:256 * (m + 1)],
                     qb[:, 128 * ci:128 * (ci + 1)], identb[:, :])
        dst = jg if w == 0 else ju
        jfree = ev_jg_free if w == 0 else ev_ju_free
        jsl = Jt % 2
        eva = v(lambda e, d=dst[jsl][:, 1024 * k:1024 * (k + 1)], s=stg[psl]:
                e.tensor_copy(d, s),
                after=[evt] + ([jfree[jsl]] if k == 0 else []))
        ev_stg_free[psl] = eva
        evac_ev[(Jt, w, k)] = eva
        if k == 3:
            gu["qbw_free"][ia % 6] = evt
            gu["qbw_free"][ib % 6] = evt

    for _ in range(2):
        emit_gu_unit()
    for w in range(2):
        for k in range(4):
            emit_T_batch(0, w, k)
    for _ in range(2):
        emit_gu_unit()

    ev_pg_free = [None, None]
    ev_pu_free = [None, None]
    ev_sil_free = [None, None]
    ev_trow = [None] * OT
    for J in range(NJQ):
        jsl = J % 2
        ev_glast, ev_ulast = None, None
        for _ in range(2):
            emit_gu_unit()
        for pb, t in enumerate((0, 1, 2, 3, 4, 5, 6, 7)):
            ts_ = t % 2
            pg = PBIG[:, 1024 + 512 * ts_:1024 + 512 * ts_ + 256]
            pu = PBIG[:, 2048 + 512 * ts_:2048 + 512 * ts_ + 256]
            for ci in range(NC):
                aft = []
                if t == 0 and ci % 4 == 0:
                    aft.append(evac_ev[(J, 0, ci // 4)])
                if ci == 0:
                    aft.append(ev_pg_free[ts_])
                    if J == 0 and t == 0:
                        aft += ev_qx_all
                ev_glast = mm(pg, qxfm[:, TOK * ci + 128 * t:TOK * ci + 128 * (t + 1)],
                              jg[jsl][:, 256 * ci:256 * (ci + 1)],
                              start=(ci == 0), stop=(ci == NC - 1), after=aft)
            for ci in range(NC):
                aft = []
                if t == 0 and ci % 4 == 0:
                    aft.append(evac_ev[(J, 1, ci // 4)])
                if ci == 0:
                    aft.append(ev_pu_free[ts_])
                ev_ulast = mm(pu, qxfm[:, TOK * ci + 128 * t:TOK * ci + 128 * (t + 1)],
                              ju[jsl][:, 256 * ci:256 * (ci + 1)],
                              start=(ci == 0), stop=(ci == NC - 1), after=aft)
            ev_sil = a(lambda e, ts_=ts_, pg=pg: e.activation(
                silstg[:, 256 * ts_:256 * (ts_ + 1)], pg, AF.Silu),
                after=[ev_glast, ev_sil_free[ts_]])
            ev_tt = v(lambda e, t=t, J=J, ts_=ts_, pu=pu: e.tensor_tensor(
                tstore[:, II * t + 256 * J:II * t + 256 * (J + 1)],
                silstg[:, 256 * ts_:256 * (ts_ + 1)], pu, ALU.mult),
                after=[ev_sil, ev_ulast, ev_trow[t]])
            ev_pg_free[ts_] = ev_sil
            ev_pu_free[ts_] = ev_tt
            ev_sil_free[ts_] = ev_tt
            ev_trow[t] = ev_tt
            if J + 1 < NJQ:
                emit_T_batch(J + 1, pb // 4, pb % 4)
            if pb == 4:
                emit_gu_unit()
        ev_jg_free[jsl] = ev_glast
        ev_ju_free[jsl] = ev_ulast
        emit_gu_unit()
    while gu["i"] < NGU:
        emit_gu_unit()

    ev_q8t_free = None
    qt_st = [None] * OT
    for t in range(OT):
        ev_rm = v(lambda e, t=t: e.reduce_max(stats[:, 44:45],
                                              tstore[:, II * t:II * (t + 1)], AX,
                                              apply_absolute_value=True),
                  after=[ev_trow[t], ev_q8t_free])
        ev_rc = v(lambda e: e.reciprocal(stats[:, 45:46], stats[:, 44:45]), after=[ev_rm])
        ev_rc2 = v(lambda e: e.tensor_scalar(stats[:, 46:47], stats[:, 45:46], 224.0, None,
                                             ALU.mult), after=[ev_rc])
        ev_dq = v(lambda e: e.tensor_scalar(stats[:, 47:48], stats[:, 44:45], 1.0 / 224.0,
                                            None, ALU.mult), after=[ev_rm])
        ev_q8 = a(lambda e, t=t: e.activation(q8t[:, :], tstore[:, II * t:II * (t + 1)],
                                              AF.Identity, bias=c_zero, scale=stats[:, 46:47]),
                  after=[ev_rc2, ev_q8t_free])
        ev_db = a(lambda e, t=t: e.activation(tstore[:, II * t:II * (t + 1)], q8t[:, :],
                                              AF.Identity, bias=c_zero, scale=stats[:, 47:48]),
                  after=[ev_q8, ev_dq])
        ev_q8t_free = ev_db
        qt_st[t] = dma("sync", qt_nat[128 * t:128 * (t + 1), :],
                       tstore[:, II * t:II * (t + 1)], after=[ev_db])
    ev_ph5_last = qt_st[-1]

    p.mark("PH5")
    es_qx.close()
    es4.close()
    es_ps4.close()
    wdres = es5.enter_context(nc.sbuf_tensor("wdres", [128, 22 * 2048], BF16))
    qtsT = es5.enter_context(nc.sbuf_tensor("qtsT", [128, 22 * 1024], BF16))
    pb67 = es_ps6.enter_context(nc.psum_tensor("pb67", [128, 1024], F32))

    ev_out = None
    ev_psum_free = None
    ev_r0_free = None
    ev_r1_free = None
    ev_wd_prev = [ev_ph5_last]
    for jh in range(2):
        ev_wd, ev_qtl = [], []
        for cj in range(22):
            cjg = 22 * jh + cj
            ev_wd.append(p.dma("sync", lambda e, cj=cj, cjg=cjg: e.dma_start(
                out=wdres[:, 2048 * cj:2048 * (cj + 1)],
                in_=qwd_scr[:, 128 * cjg:128 * (cjg + 1)], transpose=True),
                after=(dstore + qt_st + ev_wd_prev) if cj == 0 else ()))
            ev_qtl.append(p.dma("sync", lambda e, cj=cj, cjg=cjg: e.dma_start(
                out=qtsT[:, 1024 * cj:1024 * (cj + 1)],
                in_=qt_nat[:, 128 * cjg:128 * (cjg + 1)], transpose=True),
                after=(qt_st + ev_wd_prev) if cj == 0 else ()))
        ev_pfree = [ev_psum_free, ev_psum_free]
        for t in range(OT):
            todd = t % 2
            ev_mmd = None
            for cj in range(22):
                aft = []
                if t == 0:
                    aft += [ev_wd[cj], ev_qtl[cj]]
                if cj == 0:
                    aft.append(ev_pfree[todd])
                lhs = qtsT[:, 1024 * cj + 128 * t:1024 * cj + 128 * (t + 1)]
                if todd == 0:
                    ev_mmd = mmN(PBIG[:, 0:2048], lhs,
                                 wdres[:, 2048 * cj:2048 * (cj + 1)], 2048,
                                 start=(cj == 0), stop=(cj == 21), after=aft)
                else:
                    mmN(PBIG[:, 2048:3072], lhs,
                        wdres[:, 2048 * cj:2048 * cj + 1024], 1024,
                        start=(cj == 0), stop=(cj == 21), after=aft)
                    ev_mmd = mmN(pb67[:, :], lhs,
                                 wdres[:, 2048 * cj + 1024:2048 * (cj + 1)], 1024,
                                 start=(cj == 0), stop=(cj == 21))
            ev_xl = dma("sync", R(0), x2scr[128 * t:128 * (t + 1), :],
                        after=[ev_r0_free, ev_mmd])
            if todd == 0:
                ev_add = v(lambda e: e.tensor_tensor(R(1), PBIG[:, 0:2048], R(0), ALU.add),
                           after=[ev_mmd, ev_xl, ev_r1_free])
            else:
                ev_a1 = v(lambda e: e.tensor_tensor(R(1)[:, 0:1024], PBIG[:, 2048:3072],
                                                    R(0)[:, 0:1024], ALU.add),
                          after=[ev_mmd, ev_xl, ev_r1_free])
                ev_add = v(lambda e: e.tensor_tensor(R(1)[:, 1024:2048], pb67[:, :],
                                                     R(0)[:, 1024:2048], ALU.add),
                           after=[ev_a1])
            ev_pfree[todd] = ev_add
            ev_r0_free = ev_add
            if jh == 0:
                ev_w = dma("sync", x2scr[128 * t:128 * (t + 1), :], R(1), after=[ev_add])
            else:
                ev_w = dma("sync", out[128 * t:128 * (t + 1), :], R(1), after=[ev_add])
                ev_out = ev_w
            ev_r1_free = ev_w
        ev_psum_free = ev_pfree[1]
        ev_wd_prev = [ev_mmd]

    p.mark("PH6")
    p.barrier("sync", [(s, p.cnt[s]) for s in SEMS if s.startswith("sync")]
              + [(s, p.cnt[s]) for s in SEMS
                 if s.startswith(("gpsimd", "scalarq")) and p.cnt[s]])

    sems = {}
    with ExitStack() as esS:
        for e in SEMS:
            sems[e] = esS.enter_context(nc.semaphore("sem_" + e))
        block = esS.enter_context(nc.Block())

        def mk(engname):
            def f(eng):
                for seq, fn, waits, sem, amt in p.items[engname]:
                    if LIMIT is not None and seq > LIMIT:
                        continue
                    for s, vv in waits:
                        eng.wait_ge(sems[s], vv)
                    if fn is not None:
                        fn(eng).then_inc(sems[sem], amt)
            return f

        block.tensor(mk("tensor"))
        block.vector(mk("vector"))
        block.scalar(mk("scalar"))
        block.gpsimd(mk("gpsimd"))
        block.sync(mk("sync"))

    es5.close()
    es_ps6.close()
    es_ps3.close()
    es_psB.close()
    es0.close()
    return nc


_NC_CACHE = []


def kernel(L0_attn_x, L0_attn_Wq, L0_attn_Wkv, L0_attn_Wkvh, L0_attn_Wg, L0_attn_Wout,
           L0_ffn_w_gate, L0_ffn_w_up, L0_ffn_w_down, L0_ffn_up_reorder_index,
           L0_ffn_down_reorder_index):
    bf = ml_dtypes.bfloat16
    x = np.ascontiguousarray(np.asarray(L0_attn_x))
    wqT = np.ascontiguousarray(np.asarray(L0_attn_Wq).T)
    wkvT = np.ascontiguousarray(np.asarray(L0_attn_Wkv).T)
    wkvhgT = np.ascontiguousarray(
        np.concatenate([np.asarray(L0_attn_Wkvh), np.asarray(L0_attn_Wg)], axis=0).T)
    woutT = np.ascontiguousarray(np.asarray(L0_attn_Wout).T)
    wg_nat = np.ascontiguousarray(np.asarray(L0_ffn_w_gate))
    wu_nat = np.ascontiguousarray(np.asarray(L0_ffn_w_up))
    dn_idx = np.asarray(L0_ffn_down_reorder_index).astype(np.int64)
    _ = np.asarray(L0_ffn_up_reorder_index)
    wd_nat = np.ascontiguousarray(np.asarray(L0_ffn_w_down)[:, np.argsort(dn_idx)])

    inv = (np.float32(10000.0) ** (-(np.arange(0, D, 2, dtype=np.float32)) / np.float32(D)))
    inv = inv.astype(np.float32)
    pos = np.arange(N, dtype=np.float32)
    ang = (pos[:, None] * inv[None, :]).astype(np.float32)
    cosT = np.ascontiguousarray(np.cos(ang).astype(np.float32).T)
    sinT = np.ascontiguousarray(np.sin(ang).astype(np.float32).T)
    qscale = np.float32(1.0 / np.sqrt(np.float32(D)))
    t_qc = (cosT * qscale).astype(bf)
    t_qs = (sinT * qscale).astype(bf)
    t_kc = cosT.astype(bf)
    t_ks = sinT.astype(bf)

    identf = np.eye(128, dtype=np.float32)
    diagm = np.triu(np.ones((128, 128), dtype=np.float32)).astype(bf)
    onescol = np.ones((128, 1), dtype=np.float32).astype(bf)
    onesrow = np.ones((1, 128), dtype=np.float32)

    common = dict(wqT=wqT, wkvT=wkvT, wkvhgT=wkvhgT, woutT=woutT,
                  wg_nat=wg_nat, wu_nat=wu_nat, wd_nat=wd_nat,
                  t_qc=t_qc, t_qs=t_qs, t_kc=t_kc, t_ks=t_ks,
                  identf=identf, diagmask=diagm,
                  onescol=onescol, onesrow=onesrow)
    in_maps = []
    for k in range(8):
        c, s = k // 2, k % 2
        selvec = np.zeros((128, 2), dtype=np.float32)
        selvec[:, s] = 1.0
        m = dict(common)
        m["xseq"] = np.ascontiguousarray(x[c])
        m["xown"] = np.ascontiguousarray(x[c, s * TOK:(s + 1) * TOK])
        m["selvec"] = selvec
        in_maps.append(m)

    if not _NC_CACHE:
        _NC_CACHE.append(build_nc())
    nc = _NC_CACHE[0]
    res = run_bass_kernel_spmd(nc, in_maps, core_ids=list(range(8)))
    outp = np.empty((B, N, E), dtype=np.float32)
    for k in range(8):
        c, s = k // 2, k % 2
        outp[c, s * TOK:(s + 1) * TOK] = res.results[k]["out"]
    return outp


# version 18
# speedup vs baseline: 1.0102x; 1.0102x over previous
import numpy as np
from contextlib import ExitStack

import ml_dtypes
import concourse.bass as bass
import concourse.mybir as mybir
from concourse.bass_utils import run_bass_kernel_spmd

F32 = mybir.dt.float32
BF16 = mybir.dt.bfloat16
FP8 = mybir.dt.float8e4
AF = mybir.ActivationFunctionType
ALU = mybir.AluOpType
AX = mybir.AxisListType.X

B, N, E, H, D, II = 4, 2048, 2048, 16, 128, 5632
TOK = 1024
NT = N // 128
OT = TOK // 128
NC = E // 128
NJ = II // 128
CEXP = 18.0
DEBUG = False
LIMIT = None
MARKS = {}

ENGINES = ("tensor", "vector", "scalar", "gpsimd", "sync")
NPOOL = {"sync": 10, "gpsimd": 4, "scalarq": 2}
SEMS = list(ENGINES) + [f"{q}{i}" for q, n in NPOOL.items() for i in range(n)]


class Plan:
    def __init__(self):
        self.items = {e: [] for e in ENGINES}
        self.cnt = {s: 0 for s in SEMS}
        self.rr = {q: 0 for q in NPOOL}
        self.seq = 0

    def mark(self, name):
        MARKS[name] = self.seq

    def emit(self, eng, fn, after=(), sem=None, amt=1):
        sem = sem or eng
        waits = {}
        for ev in after:
            if ev is None:
                continue
            s, vv = ev
            waits[s] = max(waits.get(s, 0), vv)
        self.cnt[sem] += amt
        self.seq += 1
        self.items[eng].append((self.seq, fn, sorted(waits.items()), sem, amt))
        return (sem, self.cnt[sem])

    def dma(self, q, fn, after=()):
        sem = f"{q}{self.rr[q]}"
        self.rr[q] = (self.rr[q] + 1) % NPOOL[q]
        prev = self.cnt[sem]
        extra = ((sem, prev),) if prev else ()
        eng = "scalar" if q == "scalarq" else q
        return self.emit(eng, fn, tuple(after) + extra, sem=sem, amt=16)

    def barrier(self, eng, after):
        waits = {}
        for ev in after:
            if ev is None:
                continue
            s, vv = ev
            waits[s] = max(waits.get(s, 0), vv)
        self.seq += 1
        self.items[eng].append((self.seq, None, sorted(waits.items()), None, 0))


def build_nc():
    nc = bass.Bass("TRN2", target_bir_lowering=False)
    p = Plan()

    xseq = nc.dram_tensor("xseq", [N, E], F32, kind="ExternalInput")
    xown = nc.dram_tensor("xown", [TOK, E], F32, kind="ExternalInput")
    selv_in = nc.dram_tensor("selvec", [128, 2], F32, kind="ExternalInput")
    wqT = nc.dram_tensor("wqT", [E, E], F32, kind="ExternalInput")
    wkvT = nc.dram_tensor("wkvT", [E, 256], F32, kind="ExternalInput")
    wkvhgT = nc.dram_tensor("wkvhgT", [E, 64], F32, kind="ExternalInput")
    woutT_in = nc.dram_tensor("woutT", [E, E], F32, kind="ExternalInput")
    wg_nat = nc.dram_tensor("wg_nat", [II, E], F32, kind="ExternalInput")
    wu_nat = nc.dram_tensor("wu_nat", [II, E], F32, kind="ExternalInput")
    wd_nat = nc.dram_tensor("wd_nat", [E, II], F32, kind="ExternalInput")
    t_qc = nc.dram_tensor("t_qc", [64, N], BF16, kind="ExternalInput")
    t_qs = nc.dram_tensor("t_qs", [64, N], BF16, kind="ExternalInput")
    t_kc = nc.dram_tensor("t_kc", [64, N], BF16, kind="ExternalInput")
    t_ks = nc.dram_tensor("t_ks", [64, N], BF16, kind="ExternalInput")
    identf_in = nc.dram_tensor("identf", [128, 128], F32, kind="ExternalInput")
    diag_in = nc.dram_tensor("diagmask", [128, 128], BF16, kind="ExternalInput")
    onescol_in = nc.dram_tensor("onescol", [128, 1], BF16, kind="ExternalInput")
    onesrow_in = nc.dram_tensor("onesrow", [1, 128], F32, kind="ExternalInput")

    qwd_scr = nc.dram_tensor("qwd_scr", [E, II], BF16, kind="Internal")
    qscr = nc.dram_tensor("qscr", [E, N], BF16, kind="Internal")
    lnscr = nc.dram_tensor("lnscr", [N, E], BF16, kind="Internal")
    x2scr = nc.dram_tensor("x2scr", [TOK, E], F32, kind="Internal")
    qx_nat = nc.dram_tensor("qx_nat", [TOK, E], BF16, kind="Internal")
    qt_nat = nc.dram_tensor("qt_nat", [TOK, II], BF16, kind="Internal")
    if DEBUG:
        dbg_den = nc.dram_tensor("dbg_den", [32, 1024], F32, kind="Internal")
        dbg_ost = nc.dram_tensor("dbg_ost", [4096, 1024], BF16, kind="Internal")
        dbg_p = nc.dram_tensor("dbg_p", [128, 2048], BF16, kind="Internal")
        dbg_oo = nc.dram_tensor("dbg_oo", [128, H * 1024], BF16, kind="Internal")
        dbg_wo = nc.dram_tensor("dbg_wo", [128, NC * E], BF16, kind="Internal")
        dbg_x2 = nc.dram_tensor("dbg_x2", [TOK, E], F32, kind="Internal")
        dbg_ps = nc.dram_tensor("dbg_ps", [128, E], F32, kind="Internal")
        dbg_xs = nc.dram_tensor("dbg_xs", [128, E], F32, kind="Internal")
    out = nc.dram_tensor("out", [TOK, E], F32, kind="ExternalOutput")

    es0 = ExitStack()
    es1 = ExitStack()
    es1b = ExitStack()
    es1c = ExitStack()
    es_o = ExitStack()
    es2 = ExitStack()
    es_qx = ExitStack()
    es3 = ExitStack()
    es4 = ExitStack()
    es5 = ExitStack()
    es_ps1 = ExitStack()
    es_ps2 = ExitStack()
    es_ps3 = ExitStack()
    es_ps4 = ExitStack()
    es_ps6 = ExitStack()
    es_psB = ExitStack()

    SCR = es0.enter_context(nc.sbuf_tensor("SCR", [128, 8192], F32))
    stats = es0.enter_context(nc.sbuf_tensor("stats", [128, 64], F32))
    cvec = es0.enter_context(nc.sbuf_tensor("cvec", [128, 8], F32))
    selvec = es0.enter_context(nc.sbuf_tensor("selvec_sb", [128, 2], F32))
    identb = es0.enter_context(nc.sbuf_tensor("identb", [128, 128], BF16))

    def R(i):
        return SCR[:, 2048 * i:2048 * (i + 1)]

    ln1fm = es1.enter_context(nc.sbuf_tensor("ln1fm", [128, NC * N], BF16))
    wqbf = es1.enter_context(nc.sbuf_tensor("wqbf", [128, NC * E], BF16))
    wkvhgbf = es1.enter_context(nc.sbuf_tensor("wkvhgbf", [128, NC * 64], BF16))
    lnq = es1.enter_context(nc.sbuf_tensor("lnq", [128, 4096], BF16))
    wkvbf = lnq
    rqc = es1.enter_context(nc.sbuf_tensor("rqc", [64, N], BF16))
    rqs = es1.enter_context(nc.sbuf_tensor("rqs", [64, N], BF16))
    rkc = es1.enter_context(nc.sbuf_tensor("rkc", [64, N], BF16))
    rks = es1.enter_context(nc.sbuf_tensor("rks", [64, N], BF16))
    identf = es1.enter_context(nc.sbuf_tensor("c_identf", [128, 128], F32))

    kpfm = es1b.enter_context(nc.sbuf_tensor("kpfm", [128, N], BF16, side="right"))
    vppm = es1b.enter_context(nc.sbuf_tensor("vppm", [128, N], BF16, side="right"))
    khvh = es1b.enter_context(nc.sbuf_tensor("khvh", [128, NT * 32], F32, side="right"))

    PBIG = es_psB.enter_context(nc.psum_tensor("PBIG", [128, 3072], F32))

    def dma(q, dst, src, after=()):
        return p.dma(q, lambda e, d=dst, s=src: e.dma_start(out=d, in_=s), after)

    def mm(out_ap, lhsT, rhs, start, stop, after=()):
        return p.emit("tensor",
                      lambda e, o=out_ap, l=lhsT, r=rhs, st=start, sp=stop:
                      e.matmul(o, l, r, start=st, stop=sp, skip_group_check=True),
                      after)

    def mmN(out_ap, lhsT, rhs, ncols, start, stop, after=(), nmax=512):
        base = out_ap.offset
        ev = None
        c0 = 0
        while c0 < ncols:
            w = min(nmax, ncols - c0, 512 - ((base + c0) % 512))
            ev = mm(out_ap[:, c0:c0 + w], lhsT, rhs[:, c0:c0 + w], start, stop,
                    after if c0 == 0 else ())
            c0 += w
        return ev

    def tr(out_ap, in_ap, ident, after=()):
        return p.emit("tensor", lambda e, o=out_ap, i=in_ap, d=ident: e.transpose(o, i, d), after)

    def v(fn, after=()):
        return p.emit("vector", fn, after)

    def a(fn, after=()):
        return p.emit("scalar", fn, after)

    ev_tabs = []
    for dst, srct in ((rqc, t_qc), (rqs, t_qs), (rkc, t_kc), (rks, t_ks),
                      (identf, identf_in), (selvec, selv_in)):
        ev_tabs.append(dma("sync", dst[:, :], srct[:, :]))
    ev_consts = ev_tabs
    ev = None
    for col, val in enumerate((-CEXP, 0.0, 1.0, 1e-5, 1.0 / E, -0.5, -1.0)):
        ev = v(lambda e, c=col, x=val: e.memset(cvec[:, c:c + 1], x), after=[ev])
    ev_cvec = ev
    ev_identb = v(lambda e: e.tensor_copy(identb[:, :], identf[:, :]),
                  after=[ev_tabs[4]])
    c_bias = cvec[:, 0:1]
    c_zero = cvec[:, 1:2]
    c_one = cvec[:, 2:3]
    c_eps = cvec[:, 3:4]
    c_invE = cvec[:, 4:5]
    c_mhalf = cvec[:, 5:6]
    c_mone = cvec[:, 6:7]

    ev_wq = []
    for ci in range(NC):
        ev_wq.append(dma("gpsimd", wqbf[:, 2048 * ci:2048 * (ci + 1)],
                         wqT[128 * ci:128 * (ci + 1), :]))
    ev_wkvhg = []
    for ci in range(NC):
        ev_wkvhg.append(dma("gpsimd", wkvhgbf[:, 64 * ci:64 * (ci + 1)],
                            wkvhgT[128 * ci:128 * (ci + 1), :]))

    xslot = [R(0), R(1)]
    ev_xfree = [None, None]
    ev_sq_prev = None
    for t in range(NT):
        sl = t % 2
        ev_x = dma("sync", xslot[sl], xseq[128 * t:128 * (t + 1), :], after=[ev_xfree[sl]])
        ev_red = v(lambda e, t=t, sl=sl: e.reduce_sum(
            stats[:, 32 + t:33 + t], xslot[sl], AX), after=[ev_x])
        ev_nm = v(lambda e, t=t: e.tensor_scalar(
            stats[:, t:t + 1], stats[:, 32 + t:33 + t], -1.0 / E, None, ALU.mult),
            after=[ev_red])
        ev_sq_prev = a(lambda e, t=t, sl=sl: e.activation(
            R(2), xslot[sl], AF.Square, bias=stats[:, t:t + 1], scale=c_one,
            accum_out=stats[:, 32 + t:33 + t]),
            after=[ev_nm, ev_cvec, ev_sq_prev])
        ev_xfree[sl] = ev_sq_prev
    ev_sq_last = ev_sq_prev
    ev_lnv = a(lambda e: e.activation(stats[:, 48:64], stats[:, 32:48], AF.Ln,
                                      bias=c_eps, scale=c_invE), after=[ev_sq_last, ev_cvec])
    ev_rstd = a(lambda e: e.activation(stats[:, 16:32], stats[:, 48:64], AF.Exp,
                                       bias=c_zero, scale=c_mhalf), after=[ev_lnv])
    lnslot = [lnq[:, 0:2048], lnq[:, 2048:4096]]
    ev_lnfree = [None, None]
    ev_lnst = [None] * NT
    for t in range(NT):
        sl = t % 2
        ev_x = dma("sync", xslot[sl], xseq[128 * t:128 * (t + 1), :], after=[ev_xfree[sl]])
        ev_app = v(lambda e, t=t, sl=sl: e.tensor_scalar(
            lnslot[sl], xslot[sl], stats[:, t:t + 1], stats[:, 16 + t:17 + t],
            ALU.add, ALU.mult),
            after=[ev_x, ev_rstd, ev_lnfree[sl]])
        ev_xfree[sl] = ev_app
        ev_lnst[t] = dma("sync", lnscr[128 * t:128 * (t + 1), :], lnslot[sl],
                         after=[ev_app])
        ev_lnfree[sl] = ev_lnst[t]
    ev_ln1T = []
    for ci in range(NC):
        ev_ln1T.append(p.dma("sync",
                             lambda e, ci=ci: e.dma_start(
                                 out=ln1fm[:, N * ci:N * (ci + 1)],
                                 in_=lnscr[:, 128 * ci:128 * (ci + 1)], transpose=True),
                             after=ev_lnst if ci == 0 else ()))
    ev_ln1_done = ev_ln1T[-1]

    p.mark("PH1")
    ev_wkv = []
    for ci in range(NC):
        ev_wkv.append(dma("gpsimd", wkvbf[:, 256 * ci:256 * (ci + 1)],
                          wkvT[128 * ci:128 * (ci + 1), :],
                          after=ev_lnst if ci == 0 else ()))
    ev_kp = None
    for ci in range(NC):
        ev_kp = mmN(PBIG[:, 0:2048], wkvbf[:, 256 * ci:256 * ci + 128],
                    ln1fm[:, N * ci:N * (ci + 1)], N, start=(ci == 0), stop=(ci == NC - 1),
                    after=ev_ln1T + ev_wkv + ev_wkvhg if ci == 0 else ())
    q1, q2 = PBIG[0:64, 0:2048], PBIG[64:128, 0:2048]
    s1, s2 = R(3)[0:64, :], R(2)[0:64, :]
    s3, s4 = R(3)[64:128, :], R(2)[64:128, :]

    def rope_into(dst_sb, ctab, stab, after0):
        e1 = v(lambda e: e.tensor_tensor(s1, q1, ctab[:, :], ALU.mult), after=after0)
        e2 = v(lambda e: e.tensor_tensor(s2, q2, stab[:, :], ALU.mult), after=[e1])
        e4 = v(lambda e: e.tensor_tensor(s3, q2, ctab[:, :], ALU.mult), after=[e2])
        e5 = v(lambda e: e.tensor_tensor(s4, q1, stab[:, :], ALU.mult), after=[e4])
        e3 = v(lambda e: e.tensor_tensor(q1, s1, s2, ALU.subtract), after=[e5])
        e6 = v(lambda e: e.tensor_tensor(q2, s3, s4, ALU.add), after=[e3])
        e7 = v(lambda e, d=dst_sb: e.tensor_copy(d, PBIG[:, 0:2048]), after=[e6])
        return e7

    ev_kpfm = rope_into(kpfm[:, :], rkc, rks, [ev_kp] + ev_consts)

    ev_vp = None
    for grp in range(4):
        evl = None
        for q in range(4):
            t = 4 * grp + q
            for ci in range(NC):
                evl = mm(PBIG[:, 2048 + 128 * q:2048 + 128 * (q + 1)],
                         ln1fm[:, N * ci + 128 * t:N * ci + 128 * (t + 1)],
                         wkvbf[:, 256 * ci + 128:256 * (ci + 1)],
                         start=(ci == 0), stop=(ci == NC - 1),
                         after=ev_ln1T + [ev_vp] if (ci == 0 and q == 0) else ())
        ev_vp = v(lambda e, g=grp: e.tensor_copy(vppm[:, 512 * g:512 * (g + 1)],
                                                 PBIG[:, 2048:2560]), after=[evl])
    ev_vp_done = ev_vp

    ev_g = None
    for ci in range(NC):
        ev_g = mmN(PBIG[0:32, 0:2048], wkvhgbf[:, 64 * ci + 32:64 * (ci + 1)],
                   ln1fm[:, N * ci:N * (ci + 1)], N, start=(ci == 0), stop=(ci == NC - 1),
                   after=[ev_kpfm] if ci == 0 else ())
    fsig, zer, prod, kvh_sb = R(0)[0:32, :], R(1)[0:32, :], R(2)[0:32, :], R(3)[0:32, :]
    one_, kvhw, kvhc = R(0)[0:32, :], R(1)[0:32, :], R(3)[0:32, :]
    ev_sig = a(lambda e: e.activation(fsig, PBIG[0:32, 0:2048], AF.Sigmoid),
               after=[ev_g, ev_xfree[0], ev_xfree[1], ev_sq_last, ev_kpfm])
    ev_z = v(lambda e: e.memset(zer, 0.0), after=[ev_xfree[1], ev_kpfm])
    ev_kvhmm = None
    for ci in range(NC):
        ev_kvhmm = mmN(PBIG[0:32, 0:2048], wkvhgbf[:, 64 * ci:64 * ci + 32],
                       ln1fm[:, N * ci:N * (ci + 1)], N, start=(ci == 0), stop=(ci == NC - 1),
                       after=[ev_sig] if ci == 0 else ())
    ev_kvh = v(lambda e: e.tensor_copy(kvh_sb, PBIG[0:32, 0:2048]),
               after=[ev_kvhmm, ev_kpfm])
    ev_prod = v(lambda e: e.tensor_tensor_scan(prod, fsig, zer, 1.0, ALU.mult, ALU.add),
                after=[ev_sig, ev_z, ev_kvh, ev_sq_last])
    ev_w0 = v(lambda e: e.tensor_copy(kvhw[:, 0:1], kvh_sb[:, 0:1]), after=[ev_prod])
    ev_w1 = v(lambda e: e.tensor_tensor(kvhw[:, 1:N], kvh_sb[:, 1:N], prod[:, 0:N - 1],
                                        ALU.mult), after=[ev_w0])
    ev_o = v(lambda e: e.memset(one_, 1.0), after=[ev_prod])
    ev_kvhc = v(lambda e: e.tensor_tensor_scan(kvhc, one_, kvhw, 0.0, ALU.mult, ALU.add),
                after=[ev_w1, ev_o])
    ev_ts = [None, None]
    for t in range(NT):
        sl = t % 2
        ps = PBIG[:, 2048 + 512 * sl:2080 + 512 * sl]
        evt = tr(ps, kvhc[:, 128 * t:128 * (t + 1)], identf[0:32, 0:32],
                 after=[ev_kvhc, ev_ts[sl], ev_vp_done])
        ev_ts[sl] = v(lambda e, t=t, ps=ps: e.tensor_copy(khvh[:, 32 * t:32 * (t + 1)], ps),
                      after=[evt])
    ev_khvh = ev_ts[(NT - 1) % 2]
    ev_khvh2 = ev_ts[NT % 2]

    qstage = [lnq[:, 0:2048], lnq[:, 2048:4096]]
    ev_qst = [ev_vp_done, ev_vp_done]
    ev_qrope_scr = ev_kpfm
    ev_qmm_prev = [ev_kvh, ev_khvh, ev_khvh2]
    ev_qscr_wr = [None] * H
    for h in range(H):
        hs = h % 2
        ev_qmm = None
        for ci in range(NC):
            ev_qmm = mmN(PBIG[:, 0:2048], wqbf[:, 2048 * ci + 128 * h:2048 * ci + 128 * (h + 1)],
                         ln1fm[:, N * ci:N * (ci + 1)], N, start=(ci == 0), stop=(ci == NC - 1),
                         after=(ev_qmm_prev + ev_wq) if ci == 0 else ())
        e7 = rope_into(qstage[hs], rqc, rqs, [ev_qmm, ev_qrope_scr, ev_qst[hs]])
        ev_qrope_scr = e7
        ev_qmm_prev = [e7]
        ev_qscr_wr[h] = dma("sync", qscr[128 * h:128 * (h + 1), :], qstage[hs], after=[e7])
        ev_qst[hs] = ev_qscr_wr[h]

    p.mark("PH2")
    es1.close()
    es_ps1.close()
    es_ps2_ = es_ps2
    PDEN = es_ps2_.enter_context(nc.psum_tensor("PDEN", [1, 1024], F32))
    o_own = es_o.enter_context(nc.sbuf_tensor("o_own", [128, H * 1024], BF16))
    qfm2 = es1c.enter_context(nc.sbuf_tensor("qfm2", [128, 2 * N], BF16))
    ptile = es1c.enter_context(nc.sbuf_tensor("ptile", [128, 2048], BF16))
    vvh = es1c.enter_context(nc.sbuf_tensor("vvh", [128, 256], BF16))
    diag = es1c.enter_context(nc.sbuf_tensor("c_diag", [128, 128], BF16))
    onescol = es1c.enter_context(nc.sbuf_tensor("c_onescol", [128, 1], BF16))
    onesrow = es1c.enter_context(nc.sbuf_tensor("c_onesrow", [1, 128], F32))
    muxstg = es1c.enter_context(nc.sbuf_tensor("muxstg", [128, 1024], BF16))
    ostg = es1c.enter_context(nc.sbuf_tensor("ostg", [128, 2048], BF16))
    ev_lc = []
    for dst, srct in ((diag, diag_in), (onescol, onescol_in), (onesrow, onesrow_in)):
        ev_lc.append(dma("sync", dst[:, :], srct[:, :],
                         after=[ev_qscr_wr[H - 1]] if not ev_lc else ()))
    ev_lateconsts = ev_lc

    wnatD = es3.enter_context(nc.sbuf_tensor("wnatD", [128, II], F32))
    q8wD = es3.enter_context(nc.sbuf_tensor("q8wD", [128, II], FP8))
    qbwD = es3.enter_context(nc.sbuf_tensor("qbwD", [128, II], BF16))

    dstore = [None] * NC

    def make_ph4(jobs, wn, q8w, qbw, nslot, first):
        st = {"i": 0, "wn_free": [first[1]] * nslot, "q8w_free": first[0],
              "qbw_free": [first[1]] * nslot, "first": list(first)}

        def emit_unit():
            i = st["i"]
            if i >= len(jobs):
                return
            st["i"] = i + 1
            srcd, dstd, arr, ch, width = jobs[i]
            csl = i % nslot
            sc = 44 if csl == 0 else 56
            ev_ld = dma("sync", wn[csl][:, 0:width], srcd[128 * ch:128 * (ch + 1), :],
                        after=[st["wn_free"][csl]] + st["first"])
            st["first"] = []
            ev_rm = v(lambda e, w=width, c=csl, sc=sc: e.reduce_max(
                stats[:, sc:sc + 1], wn[c][:, 0:w], AX, apply_absolute_value=True),
                after=[ev_ld, ev_rstd])
            ev_rc = v(lambda e, sc=sc: e.reciprocal(stats[:, sc + 1:sc + 2],
                                                    stats[:, sc:sc + 1]), after=[ev_rm])
            ev_rc2 = v(lambda e, sc=sc: e.tensor_scalar(stats[:, sc + 2:sc + 3],
                                                        stats[:, sc + 1:sc + 2], 224.0, None,
                                                        ALU.mult), after=[ev_rc])
            ev_dq = v(lambda e, sc=sc: e.tensor_scalar(stats[:, sc + 3:sc + 4],
                                                       stats[:, sc:sc + 1], 1.0 / 224.0,
                                                       None, ALU.mult), after=[ev_rm])
            ev_q8 = a(lambda e, w=width, c=csl, sc=sc: e.activation(
                q8w[:, 0:w], wn[c][:, 0:w], AF.Identity, bias=c_zero,
                scale=stats[:, sc + 2:sc + 3]),
                after=[ev_rc2, st["q8w_free"], ev_cvec])
            ev_db = v(lambda e, w=width, c=csl, sc=sc: e.tensor_scalar(
                qbw[c][:, 0:w], q8w[:, 0:w], stats[:, sc + 3:sc + 4], None,
                ALU.mult), after=[ev_q8, ev_dq, st["qbw_free"][csl]])
            ev_st = dma("sync", dstd[128 * ch:128 * (ch + 1), :], qbw[csl][:, 0:width],
                        after=[ev_db])
            arr[ch] = ev_st
            st["wn_free"][csl] = ev_db
            st["q8w_free"] = ev_db
            st["qbw_free"][csl] = ev_st
        return st, emit_unit

    wd_jobs = [(wd_nat, qwd_scr, dstore, ch, II) for ch in range(NC)]
    ph4d_state, emit_wd_unit = make_ph4(
        wd_jobs, [wnatD[:, :]], q8wD, [qbwD[:, :]], 1,
        (ev_rstd, ev_qscr_wr[H - 1]))

    stA, stB = PBIG[:, 0:1024], PBIG[:, 1024:2048]
    Oreg = PBIG[:, 2048:3072]

    ev_dbg = [None, None]
    ev_dbg_ost = [None, None]
    ev_qld = [None, None]
    ev_qfree = [None, None]
    ev_st_free = [None, None]
    ev_pt_free = [None, None]
    ev_vvh_free = [None, None]
    ev_o_evac = None
    ev_den_free = None
    ev_rb_done = ev_qrope_scr
    ev_ostg_free = [None, None]
    ev_mux = [None]

    for h in range(min(2, H)):
        ev_qld[h % 2] = dma("sync", qfm2[:, N * (h % 2):N * (h % 2 + 1)],
                            qscr[128 * h:128 * (h + 1), :], after=[ev_qscr_wr[h]])

    for h in range(H):
        hs = h % 2
        qslot = qfm2[:, N * hs:N * (hs + 1)]
        ev_q_ready = ev_qld[hs]
        ev_on_pass = [None, None]
        for pas in range(2):
            cbase = 1024 * pas
            kts = [kt for kt in range(NT) if 128 * kt < cbase + 1024]
            ev_pv_last = None
            ev_den_last = None
            pend = [None]
            nkt = len(kts)

            def flush_pend():
                nonlocal ev_den_last, ev_pv_last
                if pend[0] is None:
                    return
                idx, ks, c0l, nc_, ev_msk, ev_vvhe = pend[0]
                pend[0] = None
                ev_den_last = mmN(PDEN[0:1, c0l:c0l + nc_], onescol[:, :],
                                  ptile[:, 1024 * ks:1024 * ks + nc_], nc_,
                                  start=(idx == 0), stop=(idx == nkt - 1),
                                  after=[ev_msk, ev_den_free])
                ev_pv_last = mmN(Oreg[:, c0l:c0l + nc_], vvh[:, 128 * ks:128 * (ks + 1)],
                                 ptile[:, 1024 * ks:1024 * ks + nc_], nc_,
                                 start=(idx == 0), stop=(idx == nkt - 1),
                                 after=[ev_msk, ev_vvhe, ev_o_evac])
                ev_pt_free[ks] = ev_pv_last
                ev_vvh_free[ks] = ev_pv_last

            for idx, kt in enumerate(kts):
                ks = kt % 2
                streg = stA if ks == 0 else stB
                c0g = max(cbase, 128 * kt)
                nc_ = cbase + 1024 - c0g
                c0l = c0g - cbase
                ev_st = mmN(streg[:, c0l:c0l + nc_], kpfm[:, 128 * kt:128 * (kt + 1)],
                            qslot[:, c0g:c0g + nc_], nc_, start=True, stop=True,
                            after=[ev_q_ready, ev_kpfm, ev_st_free[ks], ev_vp_done,
                                   ev_khvh, ev_khvh2] + ev_lateconsts)
                ev_exp = a(lambda e, ks=ks, nc_=nc_, sr=streg[:, c0l:c0l + nc_],
                           sc=khvh[:, 32 * kt + h:32 * kt + h + 1]:
                           e.activation(ptile[:, 1024 * ks:1024 * ks + nc_], sr, AF.Exp,
                                        bias=c_bias, scale=sc),
                           after=[ev_st, ev_pt_free[ks], ev_cvec])
                ev_st_free[ks] = ev_exp
                ev_msk = ev_exp
                if 128 * kt >= cbase:
                    ev_msk = v(lambda e, pt=ptile[:, 1024 * ks:1024 * ks + 128]:
                               e.tensor_tensor(pt, pt, diag[:, :], ALU.mult), after=[ev_exp])
                ev_vvhe = v(lambda e, ks=ks, kt=kt,
                            sc=khvh[:, 32 * kt + 16 + h:32 * kt + 17 + h]:
                            e.tensor_scalar(vvh[:, 128 * ks:128 * (ks + 1)],
                                            vppm[:, 128 * kt:128 * (kt + 1)], sc, None,
                                            ALU.mult),
                            after=[ev_vp_done, ev_khvh, ev_khvh2, ev_vvh_free[ks]]
                                   + ev_lateconsts)
                flush_pend()
                pend[0] = (idx, ks, c0l, nc_, ev_msk, ev_vvhe)
                if DEBUG and h == 0 and pas == 0 and kt == 0:
                    ev_pt_free[ks] = dma("sync", dbg_p[:, 0:1024],
                                         ptile[:, 0:1024], after=[ev_msk])
            flush_pend()
            if h + 2 < H and pas == 1:
                ev_qld[hs] = dma("sync", qslot, qscr[128 * (h + 2):128 * (h + 3), :],
                                 after=[ev_qscr_wr[h + 2], ev_pv_last])
                ev_qfree[hs] = ev_qld[hs]
            lnden = R(2)[0:1, 0:1024]
            rbrow = R(2)[0:1, 1024:2048]
            ev_lnd = a(lambda e, o=lnden: e.activation(o, PDEN[0:1, :], AF.Ln),
                       after=[ev_den_last, ev_sq_last, ev_kvhc])
            ev_rbr = a(lambda e, o=rbrow, i=lnden: e.activation(o, i, AF.Exp,
                                                                bias=cvec[0:1, 1:2],
                                                                scale=cvec[0:1, 6:7]),
                       after=[ev_lnd, ev_dbg[0]])
            ev_den_free = ev_lnd
            ev_rb1 = mm(PBIG[:, 0:512], onesrow[:, :], rbrow[0:1, 0:512], True, True,
                        after=[ev_rbr, ev_st_free[0]])
            ev_rb2 = mm(PBIG[:, 512:1024], onesrow[:, :], rbrow[0:1, 512:1024], True, True,
                        after=[ev_rbr])
            rbsb = R(3)[:, 0:1024]
            ev_rbe = v(lambda e: e.tensor_copy(rbsb, PBIG[:, 0:1024]),
                       after=[ev_rb1, ev_rb2, ev_rb_done])
            ev_on = v(lambda e, o=ostg[:, 1024 * pas:1024 * (pas + 1)]:
                      e.tensor_tensor(o, Oreg, rbsb, ALU.mult),
                      after=[ev_rbe, ev_pv_last, ev_mux[0], ev_ostg_free[pas],
                             ev_dbg_ost[pas]])
            ev_o_evac = ev_on
            ev_rb_done = ev_on
            ev_st_free[0] = ev_rbe
            ev_on_pass[pas] = ev_on
            if DEBUG:
                ev_dbg[0] = dma("sync", dbg_den[2 * h + pas:2 * h + pas + 1, :], rbrow,
                                after=[ev_rbr, ev_rbe])
                ev_dbg_ost[pas] = dma("sync",
                                      dbg_ost[128 * (2 * h + pas):128 * (2 * h + pas + 1), :],
                                      ostg[:, 1024 * pas:1024 * (pas + 1)], after=[ev_on])
            ev_ostg_free[pas] = ev_on
        ev_m1 = v(lambda e: e.tensor_scalar(muxstg[:, :], ostg[:, 0:1024],
                                            selvec[:, 0:1], None, ALU.mult),
                  after=[ev_on_pass[0]] + ev_consts)
        ev_m2 = v(lambda e, o=o_own[:, 1024 * h:1024 * (h + 1)]:
                  e.scalar_tensor_tensor(o, ostg[:, 1024:2048], selvec[:, 1:2], muxstg[:, :],
                                         ALU.mult, ALU.add),
                  after=[ev_m1, ev_on_pass[1]])
        ev_mux[0] = ev_m2
        ev_ostg_free[0] = ev_m2
        ev_ostg_free[1] = ev_m2
        if h + 2 >= H:
            ev_qfree[hs] = ev_pv_last
        emit_wd_unit()
    while ph4d_state["i"] < len(wd_jobs):
        emit_wd_unit()
    ev_ph4_all = ([ph4d_state["q8w_free"]] + ph4d_state["qbw_free"]
                  + ph4d_state["wn_free"] + [dstore[-1]])
    ev_attn_done = ev_mux[0]

    p.mark("ATTN")
    es3.close()
    es1c.close()
    es1b.close()
    es_ps2.close()
    woutbf = es2.enter_context(nc.sbuf_tensor("woutbf", [128, NC * E], BF16))
    qxfm = es_qx.enter_context(nc.sbuf_tensor("qxfm", [128, NC * TOK], BF16, side="right"))
    q8x = es2.enter_context(nc.sbuf_tensor("q8x", [128, 2048], FP8))
    lnstg2 = es2.enter_context(nc.sbuf_tensor("lnstg2", [128, 4096], BF16))
    lnstg = [lnstg2[:, 0:2048], lnstg2[:, 2048:4096]]

    if DEBUG:
        dma("sync", dbg_oo[:, :], o_own[:, :], after=[ev_attn_done])
    ev_wo = []
    for ci in range(NC):
        ev_wo.append(dma("gpsimd", woutbf[:, 2048 * ci:2048 * (ci + 1)],
                         woutT_in[128 * ci:128 * (ci + 1), :],
                         after=([ev_attn_done] + ev_ph4_all + ev_dbg_ost + ev_dbg)
                         if ci == 0 else ()))

    if DEBUG:
        dma("sync", dbg_wo[:, :], woutbf[:, :], after=ev_wo)
    ev_x2wr = [None] * OT
    ev_sqw = ev_rstd
    ev_xsl = [ev_attn_done, ev_attn_done]
    ev_wout_prev = [ev_attn_done]
    for T in range(OT):
        sl = T % 2
        ev_wmm = None
        for h in range(H):
            ev_wmm = mmN(PBIG[:, 0:2048], o_own[:, 1024 * h + 128 * T:1024 * h + 128 * (T + 1)],
                         woutbf[:, 2048 * h:2048 * (h + 1)], E,
                         start=(h == 0), stop=(h == H - 1),
                         after=(ev_wout_prev + [ev_wo[h], ev_attn_done]) if h == 0
                         else ([ev_wo[h]] if T == 0 else ()))
        ev_xl = dma("sync", xslot[sl], xown[128 * T:128 * (T + 1), :], after=[ev_xsl[sl]])
        if DEBUG and T == 0:
            evps = v(lambda e: e.tensor_copy(R(3), PBIG[:, 0:2048]), after=[ev_wmm, ev_rstd])
            evpsd = dma("sync", dbg_ps[:, :], R(3), after=[evps])
            evxsd = dma("sync", dbg_xs[:, :], xslot[sl], after=[ev_xl])
            ev_sqw = evpsd
            ev_xsl[sl] = evxsd
        ev_x2 = v(lambda e, xs_=xslot[sl]: e.tensor_tensor(R(2), PBIG[:, 0:2048], xs_, ALU.add),
                  after=[ev_wmm, ev_xl, ev_sqw, ev_xsl[sl]])
        ev_wout_prev = [ev_x2]
        ev_xsl[sl] = ev_x2
        ev_x2wr[T] = dma("sync", x2scr[128 * T:128 * (T + 1), :], R(2), after=[ev_x2])
        if DEBUG:
            ev_x2wr[T] = dma("sync", dbg_x2[128 * T:128 * (T + 1), :], R(2),
                             after=[ev_x2, ev_x2wr[T]])
        ev_red = v(lambda e, T=T: e.reduce_sum(stats[:, 32 + T:33 + T], R(2), AX),
                   after=[ev_x2])
        ev_nm = v(lambda e, T=T: e.tensor_scalar(stats[:, T:T + 1], stats[:, 32 + T:33 + T],
                                                 -1.0 / E, None, ALU.mult), after=[ev_red])
        ev_sqw = a(lambda e, T=T: e.activation(R(3), R(2), AF.Square, bias=stats[:, T:T + 1],
                                               scale=c_one, accum_out=stats[:, 32 + T:33 + T]),
                   after=[ev_nm, ev_sqw, ev_x2wr[T]])
    ev_lnv2 = a(lambda e: e.activation(stats[:, 48:56], stats[:, 32:40], AF.Ln,
                                       bias=c_eps, scale=c_invE), after=[ev_sqw])
    ev_rstd2 = a(lambda e: e.activation(stats[:, 16:24], stats[:, 48:56], AF.Exp,
                                        bias=c_zero, scale=c_mhalf), after=[ev_lnv2])

    scr_bf = SCR.bitcast(BF16)
    wnatG = [R(3), R(0)]
    qbwG = [scr_bf[:, 4096:6144], scr_bf[:, 6144:8192],
            scr_bf[:, 8192:10240], scr_bf[:, 10240:12288]]
    q8w0 = es2.enter_context(nc.sbuf_tensor("q8w0", [128, 2 * E], FP8))
    x2slot2 = es2.enter_context(nc.sbuf_tensor("x2slot", [128, 4096], F32))
    x2slot = [x2slot2[:, 0:2048], x2slot2[:, 2048:4096]]
    NJQ = II // 256
    NGU = 4 * NJQ
    ev_wlast = ev_wout_prev[0]
    gu = {"i": 0, "wn_free": [ev_rstd2, ev_wlast], "q8_free": [ev_wlast, ev_wlast],
          "dqg": [ev_rstd2, ev_rstd2],
          "qbw_free": [ev_wlast, ev_wlast, None, None, None, None],
          "db": [None] * NGU, "q8buf": q8w0}

    def emit_gu_unit():
        i = gu["i"]
        if i >= NGU:
            return
        gu["i"] = i + 1
        srcd = wg_nat if i % 2 == 0 else wu_nat
        ch = i // 2
        wsl = i % 2
        qsl = i % 6
        sc = 44 if wsl == 0 else 56
        q8b = gu["q8buf"][:, E * wsl:E * (wsl + 1)]
        ev_l = []
        for hh in range(2):
            ev_l.append(dma("sync", wnatG[wsl][:, 1024 * hh:1024 * (hh + 1)],
                            srcd[128 * ch:128 * (ch + 1), 1024 * hh:1024 * (hh + 1)],
                            after=[gu["wn_free"][wsl]]))
        ev_rm = v(lambda e, w=wsl, sc=sc: e.reduce_max(
            stats[:, sc:sc + 1], wnatG[w], AX, apply_absolute_value=True),
            after=ev_l + [ev_rstd])
        ev_rc = v(lambda e, sc=sc: e.reciprocal(stats[:, sc + 1:sc + 2],
                                                stats[:, sc:sc + 1]), after=[ev_rm])
        ev_rc2 = v(lambda e, sc=sc: e.tensor_scalar(stats[:, sc + 2:sc + 3],
                                                    stats[:, sc + 1:sc + 2], 224.0, None,
                                                    ALU.mult), after=[ev_rc])
        ev_dq = v(lambda e, sc=sc: e.tensor_scalar(stats[:, sc + 3:sc + 4],
                                                   stats[:, sc:sc + 1], 1.0 / 224.0,
                                                   None, ALU.mult),
                  after=[ev_rm, gu["dqg"][wsl]])
        ev_q8 = a(lambda e, w=wsl, sc=sc, qb=q8b: e.activation(
            qb, wnatG[w], AF.Identity, bias=c_zero,
            scale=stats[:, sc + 2:sc + 3]),
            after=[ev_rc2, gu["q8_free"][wsl], ev_cvec])
        ev_db = v(lambda e, qv=qbwG[qsl], sc=sc, qb=q8b: e.tensor_scalar(
            qv, qb, stats[:, sc + 3:sc + 4], None, ALU.mult),
            after=[ev_q8, ev_dq, gu["qbw_free"][qsl]])
        gu["db"][i] = ev_db
        gu["wn_free"][wsl] = ev_q8
        gu["q8_free"][wsl] = ev_db
        gu["dqg"][wsl] = ev_db

    ev_q8_free = None
    ev_lnstg_free = [None, None]
    ev_x2sl = [None, None]
    ev_qxst = [None] * OT
    for T in range(OT):
        sl = T % 2
        ev_xl = dma("sync", x2slot[sl], x2scr[128 * T:128 * (T + 1), :],
                    after=([ev_x2wr[T], ev_x2sl[sl]]
                           + (([ev_attn_done] + ev_ph4_all + ev_dbg_ost + ev_dbg)
                              if T < 2 else [])))
        ev_ln2 = v(lambda e, T=T, sl=sl: e.tensor_scalar(R(2), x2slot[sl], stats[:, T:T + 1],
                                                         stats[:, 16 + T:17 + T],
                                                         ALU.add, ALU.mult),
                   after=[ev_xl, ev_rstd2, ev_q8_free])
        ev_x2sl[sl] = ev_ln2
        if T in (1, 3):
            emit_gu_unit()
        ev_rm = v(lambda e: e.reduce_max(stats[:, 40:41], R(2), AX,
                                         apply_absolute_value=True), after=[ev_ln2])
        ev_rc = v(lambda e: e.reciprocal(stats[:, 41:42], stats[:, 40:41]), after=[ev_rm])
        ev_rc2 = v(lambda e: e.tensor_scalar(stats[:, 42:43], stats[:, 41:42], 224.0, None,
                                             ALU.mult), after=[ev_rc])
        ev_dq = v(lambda e: e.tensor_scalar(stats[:, 43:44], stats[:, 40:41], 1.0 / 224.0,
                                            None, ALU.mult), after=[ev_rm])
        ev_q8 = a(lambda e: e.activation(q8x[:, :], R(2), AF.Identity, bias=c_zero,
                                         scale=stats[:, 42:43]),
                  after=[ev_rc2, ev_q8_free])
        ev_dqx = a(lambda e, sl=sl: e.activation(lnstg[sl], q8x[:, :], AF.Identity,
                                                 bias=c_zero, scale=stats[:, 43:44]),
                   after=[ev_q8, ev_dq, ev_lnstg_free[sl]])
        ev_q8_free = ev_dqx
        ev_qxst[T] = dma("sync", qx_nat[128 * T:128 * (T + 1), :], lnstg[sl],
                         after=[ev_dqx])
        ev_lnstg_free[sl] = ev_qxst[T]
    gu["qbw_free"][2] = ev_q8_free
    gu["qbw_free"][3] = ev_q8_free
    for _ in range(2):
        emit_gu_unit()
    ev_qxT = []
    for ci in range(NC):
        ev_qxT.append(p.dma("sync",
                            lambda e, ci=ci: e.dma_start(
                                out=qxfm[:, TOK * ci:TOK * (ci + 1)],
                                in_=qx_nat[:, 128 * ci:128 * (ci + 1)], transpose=True),
                            after=ev_qxst if ci == 0 else ()))
    ev_qx_done = ev_qxT[-1]

    p.mark("WOUT")
    ev_wout_last = ev_wout_prev[0]
    es2.close()
    es_o.close()

    p.mark("PH4")
    tstore = es4.enter_context(nc.sbuf_tensor("tstore", [128, OT * II], BF16))
    jbuf = es4.enter_context(nc.sbuf_tensor("jbuf", [128, 4 * 4096], BF16))
    jg = [jbuf[:, 0:4096], jbuf[:, 4096:8192]]
    ju = [jbuf[:, 8192:12288], jbuf[:, 12288:16384]]
    silstg = es4.enter_context(nc.sbuf_tensor("silstg", [128, 512], BF16))
    q8t = es4.enter_context(nc.sbuf_tensor("q8t", [128, II], FP8))
    q8wS = es4.enter_context(nc.sbuf_tensor("q8wS", [128, 2 * E], FP8))
    qbwE = es4.enter_context(nc.sbuf_tensor("qbwE", [128, 2 * E], BF16))
    ptsg = es_ps4.enter_context(nc.psum_tensor("ptsg", [128, 2048], BF16))
    stg = [ptsg[:, 0:1024], ptsg[:, 1024:2048]]

    qbwG.append(qbwE[:, 0:E])
    qbwG.append(qbwE[:, E:2 * E])
    gu["qbw_free"][4] = ev_qxT[0]
    gu["qbw_free"][5] = ev_qxT[0]
    gu["q8buf"] = q8wS
    gu["q8_free"] = [ev_qxT[0], ev_qxT[0]]

    ev_qx_all = ev_qxT
    ev_jg_free = [None, None]
    ev_ju_free = [None, None]
    ev_stg_free = [None, None]
    evac_ev = {}
    kb = [0]

    def emit_T_batch(Jt, w, k):
        ia = 4 * Jt + (0 if w == 0 else 1)
        ib = ia + 2
        qa, qb = qbwG[ia % 6], qbwG[ib % 6]
        psl = kb[0] % 2
        kb[0] += 1
        evt = None
        for m in range(4):
            ci = 4 * k + m
            aft = [gu["db"][ia], gu["db"][ib], ev_stg_free[psl], ev_identb] \
                if m == 0 else ()
            evt = tr(stg[psl][:, 256 * m:256 * m + 128],
                     qa[:, 128 * ci:128 * (ci + 1)], identb[:, :], after=aft)
            evt = tr(stg[psl][:, 256 * m + 128:256 * (m + 1)],
                     qb[:, 128 * ci:128 * (ci + 1)], identb[:, :])
        dst = jg if w == 0 else ju
        jfree = ev_jg_free if w == 0 else ev_ju_free
        jsl = Jt % 2
        eva = v(lambda e, d=dst[jsl][:, 1024 * k:1024 * (k + 1)], s=stg[psl]:
                e.tensor_copy(d, s),
                after=[evt] + ([jfree[jsl]] if k == 0 else []))
        ev_stg_free[psl] = eva
        evac_ev[(Jt, w, k)] = eva
        if k == 3:
            gu["qbw_free"][ia % 6] = evt
            gu["qbw_free"][ib % 6] = evt

    for _ in range(2):
        emit_gu_unit()
    for w in range(2):
        for k in range(4):
            emit_T_batch(0, w, k)
    for _ in range(2):
        emit_gu_unit()

    ev_pg_free = [None, None]
    ev_pu_free = [None, None]
    ev_sil_free = [None, None]
    ev_trow = [None] * OT
    for J in range(NJQ):
        jsl = J % 2
        ev_glast, ev_ulast = None, None
        for _ in range(2):
            emit_gu_unit()
        for pb, t in enumerate((0, 1, 2, 3, 4, 5, 6, 7)):
            ts_ = t % 2
            pg = PBIG[:, 1024 + 512 * ts_:1024 + 512 * ts_ + 256]
            pu = PBIG[:, 2048 + 512 * ts_:2048 + 512 * ts_ + 256]
            for ci in range(NC):
                aft = []
                if t == 0 and ci % 4 == 0:
                    aft.append(evac_ev[(J, 0, ci // 4)])
                if ci == 0:
                    aft.append(ev_pg_free[ts_])
                    if J == 0 and t == 0:
                        aft += ev_qx_all
                ev_glast = mm(pg, qxfm[:, TOK * ci + 128 * t:TOK * ci + 128 * (t + 1)],
                              jg[jsl][:, 256 * ci:256 * (ci + 1)],
                              start=(ci == 0), stop=(ci == NC - 1), after=aft)
            for ci in range(NC):
                aft = []
                if t == 0 and ci % 4 == 0:
                    aft.append(evac_ev[(J, 1, ci // 4)])
                if ci == 0:
                    aft.append(ev_pu_free[ts_])
                ev_ulast = mm(pu, qxfm[:, TOK * ci + 128 * t:TOK * ci + 128 * (t + 1)],
                              ju[jsl][:, 256 * ci:256 * (ci + 1)],
                              start=(ci == 0), stop=(ci == NC - 1), after=aft)
            ev_sil = a(lambda e, ts_=ts_, pg=pg: e.activation(
                silstg[:, 256 * ts_:256 * (ts_ + 1)], pg, AF.Silu),
                after=[ev_glast, ev_sil_free[ts_]])
            ev_tt = v(lambda e, t=t, J=J, ts_=ts_, pu=pu: e.tensor_tensor(
                tstore[:, II * t + 256 * J:II * t + 256 * (J + 1)],
                silstg[:, 256 * ts_:256 * (ts_ + 1)], pu, ALU.mult),
                after=[ev_sil, ev_ulast, ev_trow[t]])
            ev_pg_free[ts_] = ev_sil
            ev_pu_free[ts_] = ev_tt
            ev_sil_free[ts_] = ev_tt
            ev_trow[t] = ev_tt
            if J + 1 < NJQ:
                emit_T_batch(J + 1, pb // 4, pb % 4)
            if pb == 4:
                emit_gu_unit()
        ev_jg_free[jsl] = ev_glast
        ev_ju_free[jsl] = ev_ulast
        emit_gu_unit()
    while gu["i"] < NGU:
        emit_gu_unit()

    ev_q8t_free = None
    qt_st = [None] * OT
    for t in range(OT):
        ev_rm = v(lambda e, t=t: e.reduce_max(stats[:, 44:45],
                                              tstore[:, II * t:II * (t + 1)], AX,
                                              apply_absolute_value=True),
                  after=[ev_trow[t], ev_q8t_free])
        ev_rc = v(lambda e: e.reciprocal(stats[:, 45:46], stats[:, 44:45]), after=[ev_rm])
        ev_rc2 = v(lambda e: e.tensor_scalar(stats[:, 46:47], stats[:, 45:46], 224.0, None,
                                             ALU.mult), after=[ev_rc])
        ev_dq = v(lambda e: e.tensor_scalar(stats[:, 47:48], stats[:, 44:45], 1.0 / 224.0,
                                            None, ALU.mult), after=[ev_rm])
        ev_q8 = a(lambda e, t=t: e.activation(q8t[:, :], tstore[:, II * t:II * (t + 1)],
                                              AF.Identity, bias=c_zero, scale=stats[:, 46:47]),
                  after=[ev_rc2, ev_q8t_free])
        ev_db = a(lambda e, t=t: e.activation(tstore[:, II * t:II * (t + 1)], q8t[:, :],
                                              AF.Identity, bias=c_zero, scale=stats[:, 47:48]),
                  after=[ev_q8, ev_dq])
        ev_q8t_free = ev_db
        qt_st[t] = dma("sync", qt_nat[128 * t:128 * (t + 1), :],
                       tstore[:, II * t:II * (t + 1)], after=[ev_db])
    ev_ph5_last = qt_st[-1]

    p.mark("PH5")
    es_qx.close()
    es4.close()
    es_ps4.close()
    wdres = es5.enter_context(nc.sbuf_tensor("wdres", [128, 22 * 2048], BF16))
    qtsT = es5.enter_context(nc.sbuf_tensor("qtsT", [128, 22 * 1024], BF16))
    pb67 = es_ps6.enter_context(nc.psum_tensor("pb67", [128, 1024], F32))

    ev_out = None
    ev_psum_free = None
    ev_r0_free = None
    ev_r1_free = None
    ev_wd_prev = [ev_ph5_last]
    for jh in range(2):
        ev_wd, ev_qtl = [], []
        for cj in range(22):
            cjg = 22 * jh + cj
            ev_wd.append(p.dma("sync", lambda e, cj=cj, cjg=cjg: e.dma_start(
                out=wdres[:, 2048 * cj:2048 * (cj + 1)],
                in_=qwd_scr[:, 128 * cjg:128 * (cjg + 1)], transpose=True),
                after=(dstore + qt_st + ev_wd_prev) if cj == 0 else ()))
            ev_qtl.append(p.dma("sync", lambda e, cj=cj, cjg=cjg: e.dma_start(
                out=qtsT[:, 1024 * cj:1024 * (cj + 1)],
                in_=qt_nat[:, 128 * cjg:128 * (cjg + 1)], transpose=True),
                after=(qt_st + ev_wd_prev) if cj == 0 else ()))
        ev_pfree = [ev_psum_free, ev_psum_free]
        for t in range(OT):
            todd = t % 2
            ev_mmd = None
            for cj in range(22):
                aft = []
                if t == 0:
                    aft += [ev_wd[cj], ev_qtl[cj]]
                if cj == 0:
                    aft.append(ev_pfree[todd])
                lhs = qtsT[:, 1024 * cj + 128 * t:1024 * cj + 128 * (t + 1)]
                if todd == 0:
                    ev_mmd = mmN(PBIG[:, 0:2048], lhs,
                                 wdres[:, 2048 * cj:2048 * (cj + 1)], 2048,
                                 start=(cj == 0), stop=(cj == 21), after=aft)
                else:
                    mmN(PBIG[:, 2048:3072], lhs,
                        wdres[:, 2048 * cj:2048 * cj + 1024], 1024,
                        start=(cj == 0), stop=(cj == 21), after=aft)
                    ev_mmd = mmN(pb67[:, :], lhs,
                                 wdres[:, 2048 * cj + 1024:2048 * (cj + 1)], 1024,
                                 start=(cj == 0), stop=(cj == 21))
            ev_xl = dma("sync", R(0), x2scr[128 * t:128 * (t + 1), :],
                        after=[ev_r0_free, ev_mmd])
            if todd == 0:
                ev_add = v(lambda e: e.tensor_tensor(R(1), PBIG[:, 0:2048], R(0), ALU.add),
                           after=[ev_mmd, ev_xl, ev_r1_free])
            else:
                ev_a1 = v(lambda e: e.tensor_tensor(R(1)[:, 0:1024], PBIG[:, 2048:3072],
                                                    R(0)[:, 0:1024], ALU.add),
                          after=[ev_mmd, ev_xl, ev_r1_free])
                ev_add = v(lambda e: e.tensor_tensor(R(1)[:, 1024:2048], pb67[:, :],
                                                     R(0)[:, 1024:2048], ALU.add),
                           after=[ev_a1])
            ev_pfree[todd] = ev_add
            ev_r0_free = ev_add
            if jh == 0:
                ev_w = dma("sync", x2scr[128 * t:128 * (t + 1), :], R(1), after=[ev_add])
            else:
                ev_w = dma("sync", out[128 * t:128 * (t + 1), :], R(1), after=[ev_add])
                ev_out = ev_w
            ev_r1_free = ev_w
        ev_psum_free = ev_pfree[1]
        ev_wd_prev = [ev_mmd]

    p.mark("PH6")
    p.barrier("sync", [(s, p.cnt[s]) for s in SEMS if s.startswith("sync")]
              + [(s, p.cnt[s]) for s in SEMS
                 if s.startswith(("gpsimd", "scalarq")) and p.cnt[s]])

    sems = {}
    with ExitStack() as esS:
        for e in SEMS:
            sems[e] = esS.enter_context(nc.semaphore("sem_" + e))
        block = esS.enter_context(nc.Block())

        def mk(engname):
            def f(eng):
                for seq, fn, waits, sem, amt in p.items[engname]:
                    if LIMIT is not None and seq > LIMIT:
                        continue
                    for s, vv in waits:
                        eng.wait_ge(sems[s], vv)
                    if fn is not None:
                        fn(eng).then_inc(sems[sem], amt)
            return f

        block.tensor(mk("tensor"))
        block.vector(mk("vector"))
        block.scalar(mk("scalar"))
        block.gpsimd(mk("gpsimd"))
        block.sync(mk("sync"))

    es5.close()
    es_ps6.close()
    es_ps3.close()
    es_psB.close()
    es0.close()
    return nc


_NC_CACHE = []


def kernel(L0_attn_x, L0_attn_Wq, L0_attn_Wkv, L0_attn_Wkvh, L0_attn_Wg, L0_attn_Wout,
           L0_ffn_w_gate, L0_ffn_w_up, L0_ffn_w_down, L0_ffn_up_reorder_index,
           L0_ffn_down_reorder_index):
    bf = ml_dtypes.bfloat16
    x = np.ascontiguousarray(np.asarray(L0_attn_x))
    wqT = np.ascontiguousarray(np.asarray(L0_attn_Wq).T)
    wkvT = np.ascontiguousarray(np.asarray(L0_attn_Wkv).T)
    wkvhgT = np.ascontiguousarray(
        np.concatenate([np.asarray(L0_attn_Wkvh), np.asarray(L0_attn_Wg)], axis=0).T)
    woutT = np.ascontiguousarray(np.asarray(L0_attn_Wout).T)
    wg_nat = np.ascontiguousarray(np.asarray(L0_ffn_w_gate))
    wu_nat = np.ascontiguousarray(np.asarray(L0_ffn_w_up))
    dn_idx = np.asarray(L0_ffn_down_reorder_index).astype(np.int64)
    _ = np.asarray(L0_ffn_up_reorder_index)
    wd_nat = np.ascontiguousarray(np.asarray(L0_ffn_w_down)[:, np.argsort(dn_idx)])

    inv = (np.float32(10000.0) ** (-(np.arange(0, D, 2, dtype=np.float32)) / np.float32(D)))
    inv = inv.astype(np.float32)
    pos = np.arange(N, dtype=np.float32)
    ang = (pos[:, None] * inv[None, :]).astype(np.float32)
    cosT = np.ascontiguousarray(np.cos(ang).astype(np.float32).T)
    sinT = np.ascontiguousarray(np.sin(ang).astype(np.float32).T)
    qscale = np.float32(1.0 / np.sqrt(np.float32(D)))
    t_qc = (cosT * qscale).astype(bf)
    t_qs = (sinT * qscale).astype(bf)
    t_kc = cosT.astype(bf)
    t_ks = sinT.astype(bf)

    identf = np.eye(128, dtype=np.float32)
    diagm = np.triu(np.ones((128, 128), dtype=np.float32)).astype(bf)
    onescol = np.ones((128, 1), dtype=np.float32).astype(bf)
    onesrow = np.ones((1, 128), dtype=np.float32)

    common = dict(wqT=wqT, wkvT=wkvT, wkvhgT=wkvhgT, woutT=woutT,
                  wg_nat=wg_nat, wu_nat=wu_nat, wd_nat=wd_nat,
                  t_qc=t_qc, t_qs=t_qs, t_kc=t_kc, t_ks=t_ks,
                  identf=identf, diagmask=diagm,
                  onescol=onescol, onesrow=onesrow)
    in_maps = []
    for k in range(8):
        c, s = k // 2, k % 2
        selvec = np.zeros((128, 2), dtype=np.float32)
        selvec[:, s] = 1.0
        m = dict(common)
        m["xseq"] = np.ascontiguousarray(x[c])
        m["xown"] = np.ascontiguousarray(x[c, s * TOK:(s + 1) * TOK])
        m["selvec"] = selvec
        in_maps.append(m)

    if not _NC_CACHE:
        _NC_CACHE.append(build_nc())
    nc = _NC_CACHE[0]
    res = run_bass_kernel_spmd(nc, in_maps, core_ids=list(range(8)))
    outp = np.empty((B, N, E), dtype=np.float32)
    for k in range(8):
        c, s = k // 2, k % 2
        outp[c, s * TOK:(s + 1) * TOK] = res.results[k]["out"]
    return outp


# version 19 (final)
# speedup vs baseline: 1.0120x; 1.0018x over previous
import numpy as np
from contextlib import ExitStack

import ml_dtypes
import concourse.bass as bass
import concourse.mybir as mybir
from concourse.bass_utils import run_bass_kernel_spmd

F32 = mybir.dt.float32
BF16 = mybir.dt.bfloat16
FP8 = mybir.dt.float8e4
AF = mybir.ActivationFunctionType
ALU = mybir.AluOpType
AX = mybir.AxisListType.X

B, N, E, H, D, II = 4, 2048, 2048, 16, 128, 5632
TOK = 1024
NT = N // 128
OT = TOK // 128
NC = E // 128
NJ = II // 128
CEXP = 18.0
DEBUG = False
LIMIT = None
MARKS = {}

ENGINES = ("tensor", "vector", "scalar", "gpsimd", "sync")
NPOOL = {"sync": 10, "gpsimd": 4, "scalarq": 2}
SEMS = list(ENGINES) + [f"{q}{i}" for q, n in NPOOL.items() for i in range(n)]


class Plan:
    def __init__(self):
        self.items = {e: [] for e in ENGINES}
        self.cnt = {s: 0 for s in SEMS}
        self.rr = {q: 0 for q in NPOOL}
        self.seq = 0

    def mark(self, name):
        MARKS[name] = self.seq

    def emit(self, eng, fn, after=(), sem=None, amt=1):
        sem = sem or eng
        waits = {}
        for ev in after:
            if ev is None:
                continue
            s, vv = ev
            waits[s] = max(waits.get(s, 0), vv)
        self.cnt[sem] += amt
        self.seq += 1
        self.items[eng].append((self.seq, fn, sorted(waits.items()), sem, amt))
        return (sem, self.cnt[sem])

    def dma(self, q, fn, after=()):
        sem = f"{q}{self.rr[q]}"
        self.rr[q] = (self.rr[q] + 1) % NPOOL[q]
        prev = self.cnt[sem]
        extra = ((sem, prev),) if prev else ()
        eng = "scalar" if q == "scalarq" else q
        return self.emit(eng, fn, tuple(after) + extra, sem=sem, amt=16)

    def barrier(self, eng, after):
        waits = {}
        for ev in after:
            if ev is None:
                continue
            s, vv = ev
            waits[s] = max(waits.get(s, 0), vv)
        self.seq += 1
        self.items[eng].append((self.seq, None, sorted(waits.items()), None, 0))


def build_nc():
    nc = bass.Bass("TRN2", target_bir_lowering=False)
    p = Plan()

    xseq = nc.dram_tensor("xseq", [N, E], F32, kind="ExternalInput")
    xown = nc.dram_tensor("xown", [TOK, E], F32, kind="ExternalInput")
    selv_in = nc.dram_tensor("selvec", [128, 2], F32, kind="ExternalInput")
    wqT = nc.dram_tensor("wqT", [E, E], F32, kind="ExternalInput")
    wkvT = nc.dram_tensor("wkvT", [E, 256], F32, kind="ExternalInput")
    wkvhgT = nc.dram_tensor("wkvhgT", [E, 64], F32, kind="ExternalInput")
    woutT_in = nc.dram_tensor("woutT", [E, E], F32, kind="ExternalInput")
    wg_nat = nc.dram_tensor("wg_nat", [II, E], F32, kind="ExternalInput")
    wu_nat = nc.dram_tensor("wu_nat", [II, E], F32, kind="ExternalInput")
    wd_nat = nc.dram_tensor("wd_nat", [E, II], F32, kind="ExternalInput")
    t_qc = nc.dram_tensor("t_qc", [64, N], BF16, kind="ExternalInput")
    t_qs = nc.dram_tensor("t_qs", [64, N], BF16, kind="ExternalInput")
    t_kc = nc.dram_tensor("t_kc", [64, N], BF16, kind="ExternalInput")
    t_ks = nc.dram_tensor("t_ks", [64, N], BF16, kind="ExternalInput")
    identf_in = nc.dram_tensor("identf", [128, 128], F32, kind="ExternalInput")
    diag_in = nc.dram_tensor("diagmask", [128, 128], BF16, kind="ExternalInput")
    onescol_in = nc.dram_tensor("onescol", [128, 1], BF16, kind="ExternalInput")
    onesrow_in = nc.dram_tensor("onesrow", [1, 128], F32, kind="ExternalInput")

    qwd_scr = nc.dram_tensor("qwd_scr", [E, II], BF16, kind="Internal")
    qscr = nc.dram_tensor("qscr", [E, N], BF16, kind="Internal")
    lnscr = nc.dram_tensor("lnscr", [N, E], BF16, kind="Internal")
    x2scr = nc.dram_tensor("x2scr", [TOK, E], F32, kind="Internal")
    qx_nat = nc.dram_tensor("qx_nat", [TOK, E], BF16, kind="Internal")
    qt_nat = nc.dram_tensor("qt_nat", [TOK, II], BF16, kind="Internal")
    if DEBUG:
        dbg_den = nc.dram_tensor("dbg_den", [32, 1024], F32, kind="Internal")
        dbg_ost = nc.dram_tensor("dbg_ost", [4096, 1024], BF16, kind="Internal")
        dbg_p = nc.dram_tensor("dbg_p", [128, 2048], BF16, kind="Internal")
        dbg_oo = nc.dram_tensor("dbg_oo", [128, H * 1024], BF16, kind="Internal")
        dbg_wo = nc.dram_tensor("dbg_wo", [128, NC * E], BF16, kind="Internal")
        dbg_x2 = nc.dram_tensor("dbg_x2", [TOK, E], F32, kind="Internal")
        dbg_ps = nc.dram_tensor("dbg_ps", [128, E], F32, kind="Internal")
        dbg_xs = nc.dram_tensor("dbg_xs", [128, E], F32, kind="Internal")
    out = nc.dram_tensor("out", [TOK, E], F32, kind="ExternalOutput")

    es0 = ExitStack()
    es1 = ExitStack()
    es1b = ExitStack()
    es1c = ExitStack()
    es_o = ExitStack()
    es2 = ExitStack()
    es_qx = ExitStack()
    es3 = ExitStack()
    es4 = ExitStack()
    es5 = ExitStack()
    es_ps1 = ExitStack()
    es_ps2 = ExitStack()
    es_ps3 = ExitStack()
    es_ps4 = ExitStack()
    es_ps6 = ExitStack()
    es_psB = ExitStack()

    SCR = es0.enter_context(nc.sbuf_tensor("SCR", [128, 8192], F32))
    stats = es0.enter_context(nc.sbuf_tensor("stats", [128, 64], F32))
    cvec = es0.enter_context(nc.sbuf_tensor("cvec", [128, 8], F32))
    selvec = es0.enter_context(nc.sbuf_tensor("selvec_sb", [128, 2], F32))
    identb = es0.enter_context(nc.sbuf_tensor("identb", [128, 128], BF16))

    def R(i):
        return SCR[:, 2048 * i:2048 * (i + 1)]

    ln1fm = es1.enter_context(nc.sbuf_tensor("ln1fm", [128, NC * N], BF16))
    wqbf = es1.enter_context(nc.sbuf_tensor("wqbf", [128, NC * E], BF16))
    wkvhgbf = es1.enter_context(nc.sbuf_tensor("wkvhgbf", [128, NC * 64], BF16))
    lnq = es1.enter_context(nc.sbuf_tensor("lnq", [128, 4096], BF16))
    wkvbf = lnq
    rqc = es1.enter_context(nc.sbuf_tensor("rqc", [64, N], BF16))
    rqs = es1.enter_context(nc.sbuf_tensor("rqs", [64, N], BF16))
    rkc = es1.enter_context(nc.sbuf_tensor("rkc", [64, N], BF16))
    rks = es1.enter_context(nc.sbuf_tensor("rks", [64, N], BF16))
    identf = es1.enter_context(nc.sbuf_tensor("c_identf", [128, 128], F32))

    kpfm = es1b.enter_context(nc.sbuf_tensor("kpfm", [128, N], BF16, side="right"))
    vppm = es1b.enter_context(nc.sbuf_tensor("vppm", [128, N], BF16, side="right"))
    khvh = es1b.enter_context(nc.sbuf_tensor("khvh", [128, NT * 32], F32, side="right"))

    PBIG = es_psB.enter_context(nc.psum_tensor("PBIG", [128, 3072], F32))

    def dma(q, dst, src, after=()):
        return p.dma(q, lambda e, d=dst, s=src: e.dma_start(out=d, in_=s), after)

    def mm(out_ap, lhsT, rhs, start, stop, after=()):
        return p.emit("tensor",
                      lambda e, o=out_ap, l=lhsT, r=rhs, st=start, sp=stop:
                      e.matmul(o, l, r, start=st, stop=sp, skip_group_check=True),
                      after)

    def mmN(out_ap, lhsT, rhs, ncols, start, stop, after=(), nmax=512):
        base = out_ap.offset
        ev = None
        c0 = 0
        while c0 < ncols:
            w = min(nmax, ncols - c0, 512 - ((base + c0) % 512))
            ev = mm(out_ap[:, c0:c0 + w], lhsT, rhs[:, c0:c0 + w], start, stop,
                    after if c0 == 0 else ())
            c0 += w
        return ev

    def tr(out_ap, in_ap, ident, after=()):
        return p.emit("tensor", lambda e, o=out_ap, i=in_ap, d=ident: e.transpose(o, i, d), after)

    def v(fn, after=()):
        return p.emit("vector", fn, after)

    def a(fn, after=()):
        return p.emit("scalar", fn, after)

    ev_tabs = []
    for dst, srct in ((rqc, t_qc), (rqs, t_qs), (rkc, t_kc), (rks, t_ks),
                      (identf, identf_in), (selvec, selv_in)):
        ev_tabs.append(dma("sync", dst[:, :], srct[:, :]))
    ev_consts = ev_tabs
    ev = None
    for col, val in enumerate((-CEXP, 0.0, 1.0, 1e-5, 1.0 / E, -0.5, -1.0)):
        ev = v(lambda e, c=col, x=val: e.memset(cvec[:, c:c + 1], x), after=[ev])
    ev_cvec = ev
    ev_identb = v(lambda e: e.tensor_copy(identb[:, :], identf[:, :]),
                  after=[ev_tabs[4]])
    c_bias = cvec[:, 0:1]
    c_zero = cvec[:, 1:2]
    c_one = cvec[:, 2:3]
    c_eps = cvec[:, 3:4]
    c_invE = cvec[:, 4:5]
    c_mhalf = cvec[:, 5:6]
    c_mone = cvec[:, 6:7]

    ev_wq = []
    for ci in range(NC):
        ev_wq.append(dma("gpsimd", wqbf[:, 2048 * ci:2048 * (ci + 1)],
                         wqT[128 * ci:128 * (ci + 1), :]))
    ev_wkvhg = []
    for ci in range(NC):
        ev_wkvhg.append(dma("gpsimd", wkvhgbf[:, 64 * ci:64 * (ci + 1)],
                            wkvhgT[128 * ci:128 * (ci + 1), :]))

    xslot = [R(0), R(1)]
    ev_xfree = [None, None]
    ev_sq_prev = None
    for t in range(NT):
        sl = t % 2
        ev_x = dma("sync", xslot[sl], xseq[128 * t:128 * (t + 1), :], after=[ev_xfree[sl]])
        ev_red = v(lambda e, t=t, sl=sl: e.reduce_sum(
            stats[:, 32 + t:33 + t], xslot[sl], AX), after=[ev_x])
        ev_nm = v(lambda e, t=t: e.tensor_scalar(
            stats[:, t:t + 1], stats[:, 32 + t:33 + t], -1.0 / E, None, ALU.mult),
            after=[ev_red])
        ev_sq_prev = a(lambda e, t=t, sl=sl: e.activation(
            R(2), xslot[sl], AF.Square, bias=stats[:, t:t + 1], scale=c_one,
            accum_out=stats[:, 32 + t:33 + t]),
            after=[ev_nm, ev_cvec, ev_sq_prev])
        ev_xfree[sl] = ev_sq_prev
    ev_sq_last = ev_sq_prev
    ev_lnv = a(lambda e: e.activation(stats[:, 48:64], stats[:, 32:48], AF.Ln,
                                      bias=c_eps, scale=c_invE), after=[ev_sq_last, ev_cvec])
    ev_rstd = a(lambda e: e.activation(stats[:, 16:32], stats[:, 48:64], AF.Exp,
                                       bias=c_zero, scale=c_mhalf), after=[ev_lnv])
    lnslot = [lnq[:, 0:2048], lnq[:, 2048:4096]]
    ev_lnfree = [None, None]
    ev_lnst = [None] * NT
    for t in range(NT):
        sl = t % 2
        ev_x = dma("sync", xslot[sl], xseq[128 * t:128 * (t + 1), :], after=[ev_xfree[sl]])
        ev_app = v(lambda e, t=t, sl=sl: e.tensor_scalar(
            lnslot[sl], xslot[sl], stats[:, t:t + 1], stats[:, 16 + t:17 + t],
            ALU.add, ALU.mult),
            after=[ev_x, ev_rstd, ev_lnfree[sl]])
        ev_xfree[sl] = ev_app
        ev_lnst[t] = dma("sync", lnscr[128 * t:128 * (t + 1), :], lnslot[sl],
                         after=[ev_app])
        ev_lnfree[sl] = ev_lnst[t]
    ev_ln1T = []
    for ci in range(NC):
        ev_ln1T.append(p.dma("sync",
                             lambda e, ci=ci: e.dma_start(
                                 out=ln1fm[:, N * ci:N * (ci + 1)],
                                 in_=lnscr[:, 128 * ci:128 * (ci + 1)], transpose=True),
                             after=ev_lnst if ci == 0 else ()))
    ev_ln1_done = ev_ln1T[-1]

    p.mark("PH1")
    ev_wkv = []
    for ci in range(NC):
        ev_wkv.append(dma("gpsimd", wkvbf[:, 256 * ci:256 * (ci + 1)],
                          wkvT[128 * ci:128 * (ci + 1), :],
                          after=ev_lnst if ci == 0 else ()))
    ev_kp = None
    for ci in range(NC):
        ev_kp = mmN(PBIG[:, 0:2048], wkvbf[:, 256 * ci:256 * ci + 128],
                    ln1fm[:, N * ci:N * (ci + 1)], N, start=(ci == 0), stop=(ci == NC - 1),
                    after=[ev_ln1T[ci], ev_wkv[ci]])
    q1, q2 = PBIG[0:64, 0:2048], PBIG[64:128, 0:2048]
    s1, s2 = R(3)[0:64, :], R(2)[0:64, :]
    s3, s4 = R(3)[64:128, :], R(2)[64:128, :]

    def rope_into(dst_sb, ctab, stab, after0):
        e1 = v(lambda e: e.tensor_tensor(s1, q1, ctab[:, :], ALU.mult), after=after0)
        e2 = v(lambda e: e.tensor_tensor(s2, q2, stab[:, :], ALU.mult), after=[e1])
        e4 = v(lambda e: e.tensor_tensor(s3, q2, ctab[:, :], ALU.mult), after=[e2])
        e5 = v(lambda e: e.tensor_tensor(s4, q1, stab[:, :], ALU.mult), after=[e4])
        e3 = v(lambda e: e.tensor_tensor(q1, s1, s2, ALU.subtract), after=[e5])
        e6 = v(lambda e: e.tensor_tensor(q2, s3, s4, ALU.add), after=[e3])
        e7 = v(lambda e, d=dst_sb: e.tensor_copy(d, PBIG[:, 0:2048]), after=[e6])
        return e7

    ev_kpfm = rope_into(kpfm[:, :], rkc, rks, [ev_kp] + ev_consts)

    ev_vp = None
    for grp in range(4):
        evl = None
        for q in range(4):
            t = 4 * grp + q
            for ci in range(NC):
                evl = mm(PBIG[:, 2048 + 128 * q:2048 + 128 * (q + 1)],
                         ln1fm[:, N * ci + 128 * t:N * ci + 128 * (t + 1)],
                         wkvbf[:, 256 * ci + 128:256 * (ci + 1)],
                         start=(ci == 0), stop=(ci == NC - 1),
                         after=ev_ln1T + [ev_vp] if (ci == 0 and q == 0) else ())
        ev_vp = v(lambda e, g=grp: e.tensor_copy(vppm[:, 512 * g:512 * (g + 1)],
                                                 PBIG[:, 2048:2560]), after=[evl])
    ev_vp_done = ev_vp

    ev_g = None
    for ci in range(NC):
        ev_g = mmN(PBIG[0:32, 0:2048], wkvhgbf[:, 64 * ci + 32:64 * (ci + 1)],
                   ln1fm[:, N * ci:N * (ci + 1)], N, start=(ci == 0), stop=(ci == NC - 1),
                   after=([ev_kpfm] + ev_wkvhg) if ci == 0 else ())
    fsig, zer, prod, kvh_sb = R(0)[0:32, :], R(1)[0:32, :], R(2)[0:32, :], R(3)[0:32, :]
    one_, kvhw, kvhc = R(0)[0:32, :], R(1)[0:32, :], R(3)[0:32, :]
    ev_sig = a(lambda e: e.activation(fsig, PBIG[0:32, 0:2048], AF.Sigmoid),
               after=[ev_g, ev_xfree[0], ev_xfree[1], ev_sq_last, ev_kpfm])
    ev_z = v(lambda e: e.memset(zer, 0.0), after=[ev_xfree[1], ev_kpfm])
    ev_kvhmm = None
    for ci in range(NC):
        ev_kvhmm = mmN(PBIG[0:32, 0:2048], wkvhgbf[:, 64 * ci:64 * ci + 32],
                       ln1fm[:, N * ci:N * (ci + 1)], N, start=(ci == 0), stop=(ci == NC - 1),
                       after=[ev_sig] if ci == 0 else ())
    ev_kvh = v(lambda e: e.tensor_copy(kvh_sb, PBIG[0:32, 0:2048]),
               after=[ev_kvhmm, ev_kpfm])
    ev_prod = v(lambda e: e.tensor_tensor_scan(prod, fsig, zer, 1.0, ALU.mult, ALU.add),
                after=[ev_sig, ev_z, ev_kvh, ev_sq_last])
    ev_w0 = v(lambda e: e.tensor_copy(kvhw[:, 0:1], kvh_sb[:, 0:1]), after=[ev_prod])
    ev_w1 = v(lambda e: e.tensor_tensor(kvhw[:, 1:N], kvh_sb[:, 1:N], prod[:, 0:N - 1],
                                        ALU.mult), after=[ev_w0])
    ev_o = v(lambda e: e.memset(one_, 1.0), after=[ev_prod])
    ev_kvhc = v(lambda e: e.tensor_tensor_scan(kvhc, one_, kvhw, 0.0, ALU.mult, ALU.add),
                after=[ev_w1, ev_o])
    ev_ts = [None, None]
    for t in range(NT):
        sl = t % 2
        ps = PBIG[:, 2048 + 512 * sl:2080 + 512 * sl]
        evt = tr(ps, kvhc[:, 128 * t:128 * (t + 1)], identf[0:32, 0:32],
                 after=[ev_kvhc, ev_ts[sl], ev_vp_done])
        ev_ts[sl] = v(lambda e, t=t, ps=ps: e.tensor_copy(khvh[:, 32 * t:32 * (t + 1)], ps),
                      after=[evt])
    ev_khvh = ev_ts[(NT - 1) % 2]
    ev_khvh2 = ev_ts[NT % 2]

    qstage = [lnq[:, 0:2048], lnq[:, 2048:4096]]
    ev_qst = [ev_vp_done, ev_vp_done]
    ev_qrope_scr = ev_kpfm
    ev_qmm_prev = [ev_kvh, ev_khvh, ev_khvh2]
    ev_qscr_wr = [None] * H
    for h in range(H):
        hs = h % 2
        ev_qmm = None
        for ci in range(NC):
            ev_qmm = mmN(PBIG[:, 0:2048], wqbf[:, 2048 * ci + 128 * h:2048 * ci + 128 * (h + 1)],
                         ln1fm[:, N * ci:N * (ci + 1)], N, start=(ci == 0), stop=(ci == NC - 1),
                         after=(ev_qmm_prev + ev_wq) if ci == 0 else ())
        e7 = rope_into(qstage[hs], rqc, rqs, [ev_qmm, ev_qrope_scr, ev_qst[hs]])
        ev_qrope_scr = e7
        ev_qmm_prev = [e7]
        ev_qscr_wr[h] = dma("sync", qscr[128 * h:128 * (h + 1), :], qstage[hs], after=[e7])
        ev_qst[hs] = ev_qscr_wr[h]

    p.mark("PH2")
    es1.close()
    es_ps1.close()
    es_ps2_ = es_ps2
    PDEN = es_ps2_.enter_context(nc.psum_tensor("PDEN", [1, 1024], F32))
    o_own = es_o.enter_context(nc.sbuf_tensor("o_own", [128, H * 1024], BF16))
    qfm2 = es1c.enter_context(nc.sbuf_tensor("qfm2", [128, 2 * N], BF16))
    ptile = es1c.enter_context(nc.sbuf_tensor("ptile", [128, 2048], BF16))
    vvh = es1c.enter_context(nc.sbuf_tensor("vvh", [128, 256], BF16))
    diag = es1c.enter_context(nc.sbuf_tensor("c_diag", [128, 128], BF16))
    onescol = es1c.enter_context(nc.sbuf_tensor("c_onescol", [128, 1], BF16))
    onesrow = es1c.enter_context(nc.sbuf_tensor("c_onesrow", [1, 128], F32))
    muxstg = es1c.enter_context(nc.sbuf_tensor("muxstg", [128, 1024], BF16))
    ostg = es1c.enter_context(nc.sbuf_tensor("ostg", [128, 2048], BF16))
    ev_lc = []
    for dst, srct in ((diag, diag_in), (onescol, onescol_in), (onesrow, onesrow_in)):
        ev_lc.append(dma("sync", dst[:, :], srct[:, :],
                         after=[ev_qscr_wr[H - 1]] if not ev_lc else ()))
    ev_lateconsts = ev_lc

    wnatD = es3.enter_context(nc.sbuf_tensor("wnatD", [128, II], F32))
    q8wD = es3.enter_context(nc.sbuf_tensor("q8wD", [128, II], FP8))
    qbwD = es3.enter_context(nc.sbuf_tensor("qbwD", [128, II], BF16))

    dstore = [None] * NC

    def make_ph4(jobs, wn, q8w, qbw, nslot, first):
        st = {"i": 0, "wn_free": [first[1]] * nslot, "q8w_free": first[0],
              "qbw_free": [first[1]] * nslot, "first": list(first)}

        def emit_unit():
            i = st["i"]
            if i >= len(jobs):
                return
            st["i"] = i + 1
            srcd, dstd, arr, ch, width = jobs[i]
            csl = i % nslot
            sc = 44 if csl == 0 else 56
            ev_ld = dma("sync", wn[csl][:, 0:width], srcd[128 * ch:128 * (ch + 1), :],
                        after=[st["wn_free"][csl]] + st["first"])
            st["first"] = []
            ev_rm = v(lambda e, w=width, c=csl, sc=sc: e.reduce_max(
                stats[:, sc:sc + 1], wn[c][:, 0:w], AX, apply_absolute_value=True),
                after=[ev_ld, ev_rstd])
            ev_rc = v(lambda e, sc=sc: e.reciprocal(stats[:, sc + 1:sc + 2],
                                                    stats[:, sc:sc + 1]), after=[ev_rm])
            ev_rc2 = v(lambda e, sc=sc: e.tensor_scalar(stats[:, sc + 2:sc + 3],
                                                        stats[:, sc + 1:sc + 2], 224.0, None,
                                                        ALU.mult), after=[ev_rc])
            ev_dq = v(lambda e, sc=sc: e.tensor_scalar(stats[:, sc + 3:sc + 4],
                                                       stats[:, sc:sc + 1], 1.0 / 224.0,
                                                       None, ALU.mult), after=[ev_rm])
            ev_q8 = a(lambda e, w=width, c=csl, sc=sc: e.activation(
                q8w[:, 0:w], wn[c][:, 0:w], AF.Identity, bias=c_zero,
                scale=stats[:, sc + 2:sc + 3]),
                after=[ev_rc2, st["q8w_free"], ev_cvec])
            ev_db = v(lambda e, w=width, c=csl, sc=sc: e.tensor_scalar(
                qbw[c][:, 0:w], q8w[:, 0:w], stats[:, sc + 3:sc + 4], None,
                ALU.mult), after=[ev_q8, ev_dq, st["qbw_free"][csl]])
            ev_st = dma("sync", dstd[128 * ch:128 * (ch + 1), :], qbw[csl][:, 0:width],
                        after=[ev_db])
            arr[ch] = ev_st
            st["wn_free"][csl] = ev_db
            st["q8w_free"] = ev_db
            st["qbw_free"][csl] = ev_st
        return st, emit_unit

    wd_jobs = [(wd_nat, qwd_scr, dstore, ch, II) for ch in range(NC)]
    ph4d_state, emit_wd_unit = make_ph4(
        wd_jobs, [wnatD[:, :]], q8wD, [qbwD[:, :]], 1,
        (ev_rstd, ev_qscr_wr[H - 1]))

    stA, stB = PBIG[:, 0:1024], PBIG[:, 1024:2048]
    Oreg = PBIG[:, 2048:3072]

    ev_dbg = [None, None]
    ev_dbg_ost = [None, None]
    ev_qld = [None, None]
    ev_qfree = [None, None]
    ev_st_free = [None, None]
    ev_pt_free = [None, None]
    ev_vvh_free = [None, None]
    ev_o_evac = None
    ev_den_free = None
    ev_rb_done = ev_qrope_scr
    ev_ostg_free = [None, None]
    ev_mux = [None]

    for h in range(min(2, H)):
        ev_qld[h % 2] = dma("sync", qfm2[:, N * (h % 2):N * (h % 2 + 1)],
                            qscr[128 * h:128 * (h + 1), :], after=[ev_qscr_wr[h]])

    for h in range(H):
        hs = h % 2
        qslot = qfm2[:, N * hs:N * (hs + 1)]
        ev_q_ready = ev_qld[hs]
        ev_on_pass = [None, None]
        for pas in range(2):
            cbase = 1024 * pas
            kts = [kt for kt in range(NT) if 128 * kt < cbase + 1024]
            ev_pv_last = None
            ev_den_last = None
            pend = [None]
            nkt = len(kts)

            def flush_pend():
                nonlocal ev_den_last, ev_pv_last
                if pend[0] is None:
                    return
                idx, ks, c0l, nc_, ev_msk, ev_vvhe = pend[0]
                pend[0] = None
                ev_den_last = mmN(PDEN[0:1, c0l:c0l + nc_], onescol[:, :],
                                  ptile[:, 1024 * ks:1024 * ks + nc_], nc_,
                                  start=(idx == 0), stop=(idx == nkt - 1),
                                  after=[ev_msk, ev_den_free])
                ev_pv_last = mmN(Oreg[:, c0l:c0l + nc_], vvh[:, 128 * ks:128 * (ks + 1)],
                                 ptile[:, 1024 * ks:1024 * ks + nc_], nc_,
                                 start=(idx == 0), stop=(idx == nkt - 1),
                                 after=[ev_msk, ev_vvhe, ev_o_evac])
                ev_pt_free[ks] = ev_pv_last
                ev_vvh_free[ks] = ev_pv_last

            for idx, kt in enumerate(kts):
                ks = kt % 2
                streg = stA if ks == 0 else stB
                c0g = max(cbase, 128 * kt)
                nc_ = cbase + 1024 - c0g
                c0l = c0g - cbase
                ev_st = mmN(streg[:, c0l:c0l + nc_], kpfm[:, 128 * kt:128 * (kt + 1)],
                            qslot[:, c0g:c0g + nc_], nc_, start=True, stop=True,
                            after=[ev_q_ready, ev_kpfm, ev_st_free[ks], ev_vp_done,
                                   ev_khvh, ev_khvh2] + ev_lateconsts)
                ev_exp = a(lambda e, ks=ks, nc_=nc_, sr=streg[:, c0l:c0l + nc_],
                           sc=khvh[:, 32 * kt + h:32 * kt + h + 1]:
                           e.activation(ptile[:, 1024 * ks:1024 * ks + nc_], sr, AF.Exp,
                                        bias=c_bias, scale=sc),
                           after=[ev_st, ev_pt_free[ks], ev_cvec])
                ev_st_free[ks] = ev_exp
                ev_msk = ev_exp
                if 128 * kt >= cbase:
                    ev_msk = v(lambda e, pt=ptile[:, 1024 * ks:1024 * ks + 128]:
                               e.tensor_tensor(pt, pt, diag[:, :], ALU.mult), after=[ev_exp])
                ev_vvhe = v(lambda e, ks=ks, kt=kt,
                            sc=khvh[:, 32 * kt + 16 + h:32 * kt + 17 + h]:
                            e.tensor_scalar(vvh[:, 128 * ks:128 * (ks + 1)],
                                            vppm[:, 128 * kt:128 * (kt + 1)], sc, None,
                                            ALU.mult),
                            after=[ev_vp_done, ev_khvh, ev_khvh2, ev_vvh_free[ks]]
                                   + ev_lateconsts)
                flush_pend()
                pend[0] = (idx, ks, c0l, nc_, ev_msk, ev_vvhe)
                if DEBUG and h == 0 and pas == 0 and kt == 0:
                    ev_pt_free[ks] = dma("sync", dbg_p[:, 0:1024],
                                         ptile[:, 0:1024], after=[ev_msk])
            flush_pend()
            if h + 2 < H and pas == 1:
                ev_qld[hs] = dma("sync", qslot, qscr[128 * (h + 2):128 * (h + 3), :],
                                 after=[ev_qscr_wr[h + 2], ev_pv_last])
                ev_qfree[hs] = ev_qld[hs]
            lnden = R(2)[0:1, 0:1024]
            rbrow = R(2)[0:1, 1024:2048]
            ev_lnd = a(lambda e, o=lnden: e.activation(o, PDEN[0:1, :], AF.Ln),
                       after=[ev_den_last, ev_sq_last, ev_kvhc])
            ev_rbr = a(lambda e, o=rbrow, i=lnden: e.activation(o, i, AF.Exp,
                                                                bias=cvec[0:1, 1:2],
                                                                scale=cvec[0:1, 6:7]),
                       after=[ev_lnd, ev_dbg[0]])
            ev_den_free = ev_lnd
            ev_rb1 = mm(PBIG[:, 0:512], onesrow[:, :], rbrow[0:1, 0:512], True, True,
                        after=[ev_rbr, ev_st_free[0]])
            ev_rb2 = mm(PBIG[:, 512:1024], onesrow[:, :], rbrow[0:1, 512:1024], True, True,
                        after=[ev_rbr])
            rbsb = R(3)[:, 0:1024]
            ev_rbe = v(lambda e: e.tensor_copy(rbsb, PBIG[:, 0:1024]),
                       after=[ev_rb1, ev_rb2, ev_rb_done])
            ev_on = v(lambda e, o=ostg[:, 1024 * pas:1024 * (pas + 1)]:
                      e.tensor_tensor(o, Oreg, rbsb, ALU.mult),
                      after=[ev_rbe, ev_pv_last, ev_mux[0], ev_ostg_free[pas],
                             ev_dbg_ost[pas]])
            ev_o_evac = ev_on
            ev_rb_done = ev_on
            ev_st_free[0] = ev_rbe
            ev_on_pass[pas] = ev_on
            if DEBUG:
                ev_dbg[0] = dma("sync", dbg_den[2 * h + pas:2 * h + pas + 1, :], rbrow,
                                after=[ev_rbr, ev_rbe])
                ev_dbg_ost[pas] = dma("sync",
                                      dbg_ost[128 * (2 * h + pas):128 * (2 * h + pas + 1), :],
                                      ostg[:, 1024 * pas:1024 * (pas + 1)], after=[ev_on])
            ev_ostg_free[pas] = ev_on
        ev_m1 = v(lambda e: e.tensor_scalar(muxstg[:, :], ostg[:, 0:1024],
                                            selvec[:, 0:1], None, ALU.mult),
                  after=[ev_on_pass[0]] + ev_consts)
        ev_m2 = v(lambda e, o=o_own[:, 1024 * h:1024 * (h + 1)]:
                  e.scalar_tensor_tensor(o, ostg[:, 1024:2048], selvec[:, 1:2], muxstg[:, :],
                                         ALU.mult, ALU.add),
                  after=[ev_m1, ev_on_pass[1]])
        ev_mux[0] = ev_m2
        ev_ostg_free[0] = ev_m2
        ev_ostg_free[1] = ev_m2
        if h + 2 >= H:
            ev_qfree[hs] = ev_pv_last
        emit_wd_unit()
    while ph4d_state["i"] < len(wd_jobs):
        emit_wd_unit()
    ev_ph4_all = ([ph4d_state["q8w_free"]] + ph4d_state["qbw_free"]
                  + ph4d_state["wn_free"] + [dstore[-1]])
    ev_attn_done = ev_mux[0]

    p.mark("ATTN")
    es3.close()
    es1c.close()
    es1b.close()
    es_ps2.close()
    woutbf = es2.enter_context(nc.sbuf_tensor("woutbf", [128, NC * E], BF16))
    qxfm = es_qx.enter_context(nc.sbuf_tensor("qxfm", [128, NC * TOK], BF16, side="right"))
    q8x = es2.enter_context(nc.sbuf_tensor("q8x", [128, 2048], FP8))
    lnstg2 = es2.enter_context(nc.sbuf_tensor("lnstg2", [128, 4096], BF16))
    lnstg = [lnstg2[:, 0:2048], lnstg2[:, 2048:4096]]

    if DEBUG:
        dma("sync", dbg_oo[:, :], o_own[:, :], after=[ev_attn_done])
    ev_wo = []
    for ci in range(NC):
        ev_wo.append(dma("gpsimd", woutbf[:, 2048 * ci:2048 * (ci + 1)],
                         woutT_in[128 * ci:128 * (ci + 1), :],
                         after=([ev_attn_done] + ev_ph4_all + ev_dbg_ost + ev_dbg)
                         if ci == 0 else ()))

    if DEBUG:
        dma("sync", dbg_wo[:, :], woutbf[:, :], after=ev_wo)
    ev_x2wr = [None] * OT
    ev_sqw = ev_rstd
    ev_xsl = [ev_attn_done, ev_attn_done]
    ev_wout_prev = [ev_attn_done]
    for T in range(OT):
        sl = T % 2
        ev_wmm = None
        for h in range(H):
            ev_wmm = mmN(PBIG[:, 0:2048], o_own[:, 1024 * h + 128 * T:1024 * h + 128 * (T + 1)],
                         woutbf[:, 2048 * h:2048 * (h + 1)], E,
                         start=(h == 0), stop=(h == H - 1),
                         after=(ev_wout_prev + [ev_wo[h], ev_attn_done]) if h == 0
                         else ([ev_wo[h]] if T == 0 else ()))
        ev_xl = dma("sync", xslot[sl], xown[128 * T:128 * (T + 1), :], after=[ev_xsl[sl]])
        if DEBUG and T == 0:
            evps = v(lambda e: e.tensor_copy(R(3), PBIG[:, 0:2048]), after=[ev_wmm, ev_rstd])
            evpsd = dma("sync", dbg_ps[:, :], R(3), after=[evps])
            evxsd = dma("sync", dbg_xs[:, :], xslot[sl], after=[ev_xl])
            ev_sqw = evpsd
            ev_xsl[sl] = evxsd
        ev_x2 = v(lambda e, xs_=xslot[sl]: e.tensor_tensor(R(2), PBIG[:, 0:2048], xs_, ALU.add),
                  after=[ev_wmm, ev_xl, ev_sqw, ev_xsl[sl]])
        ev_wout_prev = [ev_x2]
        ev_xsl[sl] = ev_x2
        ev_x2wr[T] = dma("sync", x2scr[128 * T:128 * (T + 1), :], R(2), after=[ev_x2])
        if DEBUG:
            ev_x2wr[T] = dma("sync", dbg_x2[128 * T:128 * (T + 1), :], R(2),
                             after=[ev_x2, ev_x2wr[T]])
        ev_red = v(lambda e, T=T: e.reduce_sum(stats[:, 32 + T:33 + T], R(2), AX),
                   after=[ev_x2])
        ev_nm = v(lambda e, T=T: e.tensor_scalar(stats[:, T:T + 1], stats[:, 32 + T:33 + T],
                                                 -1.0 / E, None, ALU.mult), after=[ev_red])
        ev_sqw = a(lambda e, T=T: e.activation(R(3), R(2), AF.Square, bias=stats[:, T:T + 1],
                                               scale=c_one, accum_out=stats[:, 32 + T:33 + T]),
                   after=[ev_nm, ev_sqw, ev_x2wr[T]])
    ev_lnv2 = a(lambda e: e.activation(stats[:, 48:56], stats[:, 32:40], AF.Ln,
                                       bias=c_eps, scale=c_invE), after=[ev_sqw])
    ev_rstd2 = a(lambda e: e.activation(stats[:, 16:24], stats[:, 48:56], AF.Exp,
                                        bias=c_zero, scale=c_mhalf), after=[ev_lnv2])

    scr_bf = SCR.bitcast(BF16)
    wnatG = [R(3), R(0)]
    qbwG = [scr_bf[:, 4096:6144], scr_bf[:, 6144:8192],
            scr_bf[:, 8192:10240], scr_bf[:, 10240:12288]]
    q8w0 = es2.enter_context(nc.sbuf_tensor("q8w0", [128, 2 * E], FP8))
    x2slot2 = es2.enter_context(nc.sbuf_tensor("x2slot", [128, 4096], F32))
    x2slot = [x2slot2[:, 0:2048], x2slot2[:, 2048:4096]]
    NJQ = II // 256
    NGU = 4 * NJQ
    ev_wlast = ev_wout_prev[0]
    gu = {"i": 0, "wn_free": [ev_rstd2, ev_wlast], "q8_free": [ev_wlast, ev_wlast],
          "dqg": [ev_rstd2, ev_rstd2],
          "qbw_free": [ev_wlast, ev_wlast, None, None, None, None],
          "db": [None] * NGU, "q8buf": q8w0}

    def emit_gu_unit():
        i = gu["i"]
        if i >= NGU:
            return
        gu["i"] = i + 1
        srcd = wg_nat if i % 2 == 0 else wu_nat
        ch = i // 2
        wsl = i % 2
        qsl = i % 6
        sc = 44 if wsl == 0 else 56
        q8b = gu["q8buf"][:, E * wsl:E * (wsl + 1)]
        ev_l = []
        for hh in range(2):
            ev_l.append(dma("sync", wnatG[wsl][:, 1024 * hh:1024 * (hh + 1)],
                            srcd[128 * ch:128 * (ch + 1), 1024 * hh:1024 * (hh + 1)],
                            after=[gu["wn_free"][wsl]]))
        ev_rm = v(lambda e, w=wsl, sc=sc: e.reduce_max(
            stats[:, sc:sc + 1], wnatG[w], AX, apply_absolute_value=True),
            after=ev_l + [ev_rstd])
        ev_rc = v(lambda e, sc=sc: e.reciprocal(stats[:, sc + 1:sc + 2],
                                                stats[:, sc:sc + 1]), after=[ev_rm])
        ev_rc2 = v(lambda e, sc=sc: e.tensor_scalar(stats[:, sc + 2:sc + 3],
                                                    stats[:, sc + 1:sc + 2], 224.0, None,
                                                    ALU.mult), after=[ev_rc])
        ev_dq = v(lambda e, sc=sc: e.tensor_scalar(stats[:, sc + 3:sc + 4],
                                                   stats[:, sc:sc + 1], 1.0 / 224.0,
                                                   None, ALU.mult),
                  after=[ev_rm, gu["dqg"][wsl]])
        ev_q8 = a(lambda e, w=wsl, sc=sc, qb=q8b: e.activation(
            qb, wnatG[w], AF.Identity, bias=c_zero,
            scale=stats[:, sc + 2:sc + 3]),
            after=[ev_rc2, gu["q8_free"][wsl], ev_cvec])
        ev_db = v(lambda e, qv=qbwG[qsl], sc=sc, qb=q8b: e.tensor_scalar(
            qv, qb, stats[:, sc + 3:sc + 4], None, ALU.mult),
            after=[ev_q8, ev_dq, gu["qbw_free"][qsl]])
        gu["db"][i] = ev_db
        gu["wn_free"][wsl] = ev_q8
        gu["q8_free"][wsl] = ev_db
        gu["dqg"][wsl] = ev_db

    ev_q8_free = None
    ev_lnstg_free = [None, None]
    ev_x2sl = [None, None]
    ev_qxst = [None] * OT
    for T in range(OT):
        sl = T % 2
        ev_xh = [dma("sync", x2slot[sl][:, 1024 * hh:1024 * (hh + 1)],
                     x2scr[128 * T:128 * (T + 1), 1024 * hh:1024 * (hh + 1)],
                     after=([ev_x2wr[T], ev_x2sl[sl]]
                            + (([ev_attn_done] + ev_ph4_all + ev_dbg_ost + ev_dbg)
                               if T < 2 else []))) for hh in range(2)]
        ev_xl = None
        ev_ln2 = v(lambda e, T=T, sl=sl: e.tensor_scalar(R(2), x2slot[sl], stats[:, T:T + 1],
                                                         stats[:, 16 + T:17 + T],
                                                         ALU.add, ALU.mult),
                   after=ev_xh + [ev_rstd2, ev_q8_free])
        ev_x2sl[sl] = ev_ln2
        if T in (1, 3):
            emit_gu_unit()
        ev_rm = v(lambda e: e.reduce_max(stats[:, 40:41], R(2), AX,
                                         apply_absolute_value=True), after=[ev_ln2])
        ev_rc = v(lambda e: e.reciprocal(stats[:, 41:42], stats[:, 40:41]), after=[ev_rm])
        ev_rc2 = v(lambda e: e.tensor_scalar(stats[:, 42:43], stats[:, 41:42], 224.0, None,
                                             ALU.mult), after=[ev_rc])
        ev_dq = v(lambda e: e.tensor_scalar(stats[:, 43:44], stats[:, 40:41], 1.0 / 224.0,
                                            None, ALU.mult), after=[ev_rm])
        ev_q8 = a(lambda e: e.activation(q8x[:, :], R(2), AF.Identity, bias=c_zero,
                                         scale=stats[:, 42:43]),
                  after=[ev_rc2, ev_q8_free])
        ev_dqx = a(lambda e, sl=sl: e.activation(lnstg[sl], q8x[:, :], AF.Identity,
                                                 bias=c_zero, scale=stats[:, 43:44]),
                   after=[ev_q8, ev_dq, ev_lnstg_free[sl]])
        ev_q8_free = ev_dqx
        ev_qxst[T] = dma("sync", qx_nat[128 * T:128 * (T + 1), :], lnstg[sl],
                         after=[ev_dqx])
        ev_lnstg_free[sl] = ev_qxst[T]
    gu["qbw_free"][2] = ev_q8_free
    gu["qbw_free"][3] = ev_q8_free
    for _ in range(2):
        emit_gu_unit()
    ev_qxT = []
    for ci in range(NC):
        ev_qxT.append(p.dma("sync",
                            lambda e, ci=ci: e.dma_start(
                                out=qxfm[:, TOK * ci:TOK * (ci + 1)],
                                in_=qx_nat[:, 128 * ci:128 * (ci + 1)], transpose=True),
                            after=ev_qxst if ci == 0 else ()))
    ev_qx_done = ev_qxT[-1]

    p.mark("WOUT")
    ev_wout_last = ev_wout_prev[0]
    es2.close()
    es_o.close()

    p.mark("PH4")
    tstore = es4.enter_context(nc.sbuf_tensor("tstore", [128, OT * II], BF16))
    jbuf = es4.enter_context(nc.sbuf_tensor("jbuf", [128, 4 * 4096], BF16))
    jg = [jbuf[:, 0:4096], jbuf[:, 4096:8192]]
    ju = [jbuf[:, 8192:12288], jbuf[:, 12288:16384]]
    silstg = es4.enter_context(nc.sbuf_tensor("silstg", [128, 512], BF16))
    q8t = es4.enter_context(nc.sbuf_tensor("q8t", [128, II], FP8))
    q8wS = es4.enter_context(nc.sbuf_tensor("q8wS", [128, 2 * E], FP8))
    qbwE = es4.enter_context(nc.sbuf_tensor("qbwE", [128, 2 * E], BF16))
    ptsg = es_ps4.enter_context(nc.psum_tensor("ptsg", [128, 2048], BF16))
    stg = [ptsg[:, 0:1024], ptsg[:, 1024:2048]]

    qbwG.append(qbwE[:, 0:E])
    qbwG.append(qbwE[:, E:2 * E])
    gu["qbw_free"][4] = ev_qxT[0]
    gu["qbw_free"][5] = ev_qxT[0]
    gu["q8buf"] = q8wS
    gu["q8_free"] = [ev_qxT[0], ev_qxT[0]]

    ev_qx_all = ev_qxT
    ev_jg_free = [None, None]
    ev_ju_free = [None, None]
    ev_stg_free = [None, None]
    evac_ev = {}
    kb = [0]

    def emit_T_batch(Jt, w, k):
        ia = 4 * Jt + (0 if w == 0 else 1)
        ib = ia + 2
        qa, qb = qbwG[ia % 6], qbwG[ib % 6]
        psl = kb[0] % 2
        kb[0] += 1
        evt = None
        for m in range(4):
            ci = 4 * k + m
            aft = [gu["db"][ia], gu["db"][ib], ev_stg_free[psl], ev_identb] \
                if m == 0 else ()
            evt = tr(stg[psl][:, 256 * m:256 * m + 128],
                     qa[:, 128 * ci:128 * (ci + 1)], identb[:, :], after=aft)
            evt = tr(stg[psl][:, 256 * m + 128:256 * (m + 1)],
                     qb[:, 128 * ci:128 * (ci + 1)], identb[:, :])
        dst = jg if w == 0 else ju
        jfree = ev_jg_free if w == 0 else ev_ju_free
        jsl = Jt % 2
        eva = v(lambda e, d=dst[jsl][:, 1024 * k:1024 * (k + 1)], s=stg[psl]:
                e.tensor_copy(d, s),
                after=[evt] + ([jfree[jsl]] if k == 0 else []))
        ev_stg_free[psl] = eva
        evac_ev[(Jt, w, k)] = eva
        if k == 3:
            gu["qbw_free"][ia % 6] = evt
            gu["qbw_free"][ib % 6] = evt

    for _ in range(2):
        emit_gu_unit()
    for w in range(2):
        for k in range(4):
            emit_T_batch(0, w, k)
    for _ in range(2):
        emit_gu_unit()

    ev_pg_free = [None, None]
    ev_pu_free = [None, None]
    ev_sil_free = [None, None]
    ev_trow = [None] * OT
    for J in range(NJQ):
        jsl = J % 2
        ev_glast, ev_ulast = None, None
        for _ in range(2):
            emit_gu_unit()
        for pb, t in enumerate((0, 1, 2, 3, 4, 5, 6, 7)):
            ts_ = t % 2
            pg = PBIG[:, 1024 + 512 * ts_:1024 + 512 * ts_ + 256]
            pu = PBIG[:, 2048 + 512 * ts_:2048 + 512 * ts_ + 256]
            for ci in range(NC):
                aft = []
                if t == 0 and ci % 4 == 0:
                    aft.append(evac_ev[(J, 0, ci // 4)])
                if ci == 0:
                    aft.append(ev_pg_free[ts_])
                    if J == 0 and t == 0:
                        aft += ev_qx_all
                ev_glast = mm(pg, qxfm[:, TOK * ci + 128 * t:TOK * ci + 128 * (t + 1)],
                              jg[jsl][:, 256 * ci:256 * (ci + 1)],
                              start=(ci == 0), stop=(ci == NC - 1), after=aft)
            for ci in range(NC):
                aft = []
                if t == 0 and ci % 4 == 0:
                    aft.append(evac_ev[(J, 1, ci // 4)])
                if ci == 0:
                    aft.append(ev_pu_free[ts_])
                ev_ulast = mm(pu, qxfm[:, TOK * ci + 128 * t:TOK * ci + 128 * (t + 1)],
                              ju[jsl][:, 256 * ci:256 * (ci + 1)],
                              start=(ci == 0), stop=(ci == NC - 1), after=aft)
            ev_sil = a(lambda e, ts_=ts_, pg=pg: e.activation(
                silstg[:, 256 * ts_:256 * (ts_ + 1)], pg, AF.Silu),
                after=[ev_glast, ev_sil_free[ts_]])
            ev_tt = v(lambda e, t=t, J=J, ts_=ts_, pu=pu: e.tensor_tensor(
                tstore[:, II * t + 256 * J:II * t + 256 * (J + 1)],
                silstg[:, 256 * ts_:256 * (ts_ + 1)], pu, ALU.mult),
                after=[ev_sil, ev_ulast, ev_trow[t]])
            ev_pg_free[ts_] = ev_sil
            ev_pu_free[ts_] = ev_tt
            ev_sil_free[ts_] = ev_tt
            ev_trow[t] = ev_tt
            if J + 1 < NJQ:
                emit_T_batch(J + 1, pb // 4, pb % 4)
            if pb == 4:
                emit_gu_unit()
        ev_jg_free[jsl] = ev_glast
        ev_ju_free[jsl] = ev_ulast
        emit_gu_unit()
    while gu["i"] < NGU:
        emit_gu_unit()

    ev_q8t_free = None
    qt_st = [None] * OT
    for t in range(OT):
        ev_rm = v(lambda e, t=t: e.reduce_max(stats[:, 44:45],
                                              tstore[:, II * t:II * (t + 1)], AX,
                                              apply_absolute_value=True),
                  after=[ev_trow[t], ev_q8t_free])
        ev_rc = v(lambda e: e.reciprocal(stats[:, 45:46], stats[:, 44:45]), after=[ev_rm])
        ev_rc2 = v(lambda e: e.tensor_scalar(stats[:, 46:47], stats[:, 45:46], 224.0, None,
                                             ALU.mult), after=[ev_rc])
        ev_dq = v(lambda e: e.tensor_scalar(stats[:, 47:48], stats[:, 44:45], 1.0 / 224.0,
                                            None, ALU.mult), after=[ev_rm])
        ev_q8 = a(lambda e, t=t: e.activation(q8t[:, :], tstore[:, II * t:II * (t + 1)],
                                              AF.Identity, bias=c_zero, scale=stats[:, 46:47]),
                  after=[ev_rc2, ev_q8t_free])
        ev_db = a(lambda e, t=t: e.activation(tstore[:, II * t:II * (t + 1)], q8t[:, :],
                                              AF.Identity, bias=c_zero, scale=stats[:, 47:48]),
                  after=[ev_q8, ev_dq])
        ev_q8t_free = ev_db
        qt_st[t] = dma("sync", qt_nat[128 * t:128 * (t + 1), :],
                       tstore[:, II * t:II * (t + 1)], after=[ev_db])
    ev_ph5_last = qt_st[-1]

    p.mark("PH5")
    es_qx.close()
    es4.close()
    es_ps4.close()
    wdres = es5.enter_context(nc.sbuf_tensor("wdres", [128, 22 * 2048], BF16))
    qtsT = es5.enter_context(nc.sbuf_tensor("qtsT", [128, 22 * 1024], BF16))
    pb67 = es_ps6.enter_context(nc.psum_tensor("pb67", [128, 1024], F32))

    ev_out = None
    ev_psum_free = None
    ev_r0_free = None
    ev_r1_free = None
    ev_wd_prev = [ev_ph5_last]
    for jh in range(2):
        ev_wd, ev_qtl = [], []
        for cj in range(22):
            cjg = 22 * jh + cj
            ev_wd.append(p.dma("sync", lambda e, cj=cj, cjg=cjg: e.dma_start(
                out=wdres[:, 2048 * cj:2048 * (cj + 1)],
                in_=qwd_scr[:, 128 * cjg:128 * (cjg + 1)], transpose=True),
                after=(dstore + qt_st + ev_wd_prev) if cj == 0 else ()))
            ev_qtl.append(p.dma("sync", lambda e, cj=cj, cjg=cjg: e.dma_start(
                out=qtsT[:, 1024 * cj:1024 * (cj + 1)],
                in_=qt_nat[:, 128 * cjg:128 * (cjg + 1)], transpose=True),
                after=(qt_st + ev_wd_prev) if cj == 0 else ()))
        ev_pfree = [ev_psum_free, ev_psum_free]
        for t in range(OT):
            todd = t % 2
            ev_mmd = None
            for cj in range(22):
                aft = []
                if t == 0:
                    aft += [ev_wd[cj], ev_qtl[cj]]
                if cj == 0:
                    aft.append(ev_pfree[todd])
                lhs = qtsT[:, 1024 * cj + 128 * t:1024 * cj + 128 * (t + 1)]
                if todd == 0:
                    ev_mmd = mmN(PBIG[:, 0:2048], lhs,
                                 wdres[:, 2048 * cj:2048 * (cj + 1)], 2048,
                                 start=(cj == 0), stop=(cj == 21), after=aft)
                else:
                    mmN(PBIG[:, 2048:3072], lhs,
                        wdres[:, 2048 * cj:2048 * cj + 1024], 1024,
                        start=(cj == 0), stop=(cj == 21), after=aft)
                    ev_mmd = mmN(pb67[:, :], lhs,
                                 wdres[:, 2048 * cj + 1024:2048 * (cj + 1)], 1024,
                                 start=(cj == 0), stop=(cj == 21))
            ev_xl = dma("sync", R(0), x2scr[128 * t:128 * (t + 1), :],
                        after=[ev_r0_free, ev_mmd])
            if todd == 0:
                ev_add = v(lambda e: e.tensor_tensor(R(1), PBIG[:, 0:2048], R(0), ALU.add),
                           after=[ev_mmd, ev_xl, ev_r1_free])
            else:
                ev_a1 = v(lambda e: e.tensor_tensor(R(1)[:, 0:1024], PBIG[:, 2048:3072],
                                                    R(0)[:, 0:1024], ALU.add),
                          after=[ev_mmd, ev_xl, ev_r1_free])
                ev_add = v(lambda e: e.tensor_tensor(R(1)[:, 1024:2048], pb67[:, :],
                                                     R(0)[:, 1024:2048], ALU.add),
                           after=[ev_a1])
            ev_pfree[todd] = ev_add
            ev_r0_free = ev_add
            if jh == 0:
                ev_w = dma("sync", x2scr[128 * t:128 * (t + 1), :], R(1), after=[ev_add])
            else:
                ev_w = dma("sync", out[128 * t:128 * (t + 1), :], R(1), after=[ev_add])
                ev_out = ev_w
            ev_r1_free = ev_w
        ev_psum_free = ev_pfree[1]
        ev_wd_prev = [ev_mmd]

    p.mark("PH6")
    p.barrier("sync", [(s, p.cnt[s]) for s in SEMS if s.startswith("sync")]
              + [(s, p.cnt[s]) for s in SEMS
                 if s.startswith(("gpsimd", "scalarq")) and p.cnt[s]])

    sems = {}
    with ExitStack() as esS:
        for e in SEMS:
            sems[e] = esS.enter_context(nc.semaphore("sem_" + e))
        block = esS.enter_context(nc.Block())

        def mk(engname):
            def f(eng):
                for seq, fn, waits, sem, amt in p.items[engname]:
                    if LIMIT is not None and seq > LIMIT:
                        continue
                    for s, vv in waits:
                        eng.wait_ge(sems[s], vv)
                    if fn is not None:
                        fn(eng).then_inc(sems[sem], amt)
            return f

        block.tensor(mk("tensor"))
        block.vector(mk("vector"))
        block.scalar(mk("scalar"))
        block.gpsimd(mk("gpsimd"))
        block.sync(mk("sync"))

    es5.close()
    es_ps6.close()
    es_ps3.close()
    es_psB.close()
    es0.close()
    return nc


_NC_CACHE = []


def kernel(L0_attn_x, L0_attn_Wq, L0_attn_Wkv, L0_attn_Wkvh, L0_attn_Wg, L0_attn_Wout,
           L0_ffn_w_gate, L0_ffn_w_up, L0_ffn_w_down, L0_ffn_up_reorder_index,
           L0_ffn_down_reorder_index):
    bf = ml_dtypes.bfloat16
    x = np.ascontiguousarray(np.asarray(L0_attn_x))
    wqT = np.ascontiguousarray(np.asarray(L0_attn_Wq).T)
    wkvT = np.ascontiguousarray(np.asarray(L0_attn_Wkv).T)
    wkvhgT = np.ascontiguousarray(
        np.concatenate([np.asarray(L0_attn_Wkvh), np.asarray(L0_attn_Wg)], axis=0).T)
    woutT = np.ascontiguousarray(np.asarray(L0_attn_Wout).T)
    wg_nat = np.ascontiguousarray(np.asarray(L0_ffn_w_gate))
    wu_nat = np.ascontiguousarray(np.asarray(L0_ffn_w_up))
    dn_idx = np.asarray(L0_ffn_down_reorder_index).astype(np.int64)
    _ = np.asarray(L0_ffn_up_reorder_index)
    wd_nat = np.ascontiguousarray(np.asarray(L0_ffn_w_down)[:, np.argsort(dn_idx)])

    inv = (np.float32(10000.0) ** (-(np.arange(0, D, 2, dtype=np.float32)) / np.float32(D)))
    inv = inv.astype(np.float32)
    pos = np.arange(N, dtype=np.float32)
    ang = (pos[:, None] * inv[None, :]).astype(np.float32)
    cosT = np.ascontiguousarray(np.cos(ang).astype(np.float32).T)
    sinT = np.ascontiguousarray(np.sin(ang).astype(np.float32).T)
    qscale = np.float32(1.0 / np.sqrt(np.float32(D)))
    t_qc = (cosT * qscale).astype(bf)
    t_qs = (sinT * qscale).astype(bf)
    t_kc = cosT.astype(bf)
    t_ks = sinT.astype(bf)

    identf = np.eye(128, dtype=np.float32)
    diagm = np.triu(np.ones((128, 128), dtype=np.float32)).astype(bf)
    onescol = np.ones((128, 1), dtype=np.float32).astype(bf)
    onesrow = np.ones((1, 128), dtype=np.float32)

    common = dict(wqT=wqT, wkvT=wkvT, wkvhgT=wkvhgT, woutT=woutT,
                  wg_nat=wg_nat, wu_nat=wu_nat, wd_nat=wd_nat,
                  t_qc=t_qc, t_qs=t_qs, t_kc=t_kc, t_ks=t_ks,
                  identf=identf, diagmask=diagm,
                  onescol=onescol, onesrow=onesrow)
    in_maps = []
    for k in range(8):
        c, s = k // 2, k % 2
        selvec = np.zeros((128, 2), dtype=np.float32)
        selvec[:, s] = 1.0
        m = dict(common)
        m["xseq"] = np.ascontiguousarray(x[c])
        m["xown"] = np.ascontiguousarray(x[c, s * TOK:(s + 1) * TOK])
        m["selvec"] = selvec
        in_maps.append(m)

    if not _NC_CACHE:
        _NC_CACHE.append(build_nc())
    nc = _NC_CACHE[0]
    res = run_bass_kernel_spmd(nc, in_maps, core_ids=list(range(8)))
    outp = np.empty((B, N, E), dtype=np.float32)
    for k in range(8):
        c, s = k // 2, k % 2
        outp[c, s * TOK:(s + 1) * TOK] = res.results[k]["out"]
    return outp
